# version 11
# baseline (speedup 1.0000x reference)
import numpy as np
import ml_dtypes
import concourse.bass as bass
import concourse.mybir as mybir
from concourse.bass_utils import run_bass_kernel_spmd

F32 = mybir.dt.float32
BF16 = mybir.dt.bfloat16
AF = mybir.ActivationFunctionType
ALU = mybir.AluOpType
AX = mybir.AxisListType

T = 4096
C = 1024
NH = 16
HD = 64
LD = 512
SEG = 512
NSEG = T // SEG
NTT = SEG // 128
NCH = C // 128
CSH = 4.0
EPS = 1e-5

ENGINES = ("tensor", "scalar", "vector", "gpsimd", "sync")

CFG = dict(sub1=True, sub2=True, sub3=True, sub4=True, s3proj=True, s3scan=True, s3out=True)


class Op:
    __slots__ = ("eng", "fn", "args", "kwargs", "waits", "need_inc", "val", "sem", "amt", "raw", "idx")

    def __init__(self, eng, fn, args, kwargs, sem, amt, raw=False):
        self.eng, self.fn, self.args, self.kwargs = eng, fn, args, kwargs
        self.waits = set()
        self.need_inc = False
        self.val = None
        self.sem = sem
        self.amt = amt
        self.raw = raw
        self.idx = -1


class Prog:
    def __init__(self, nc):
        self.nc = nc
        self.ops = {e: [] for e in ENGINES}
        self.lastw = {}
        self.readers = {}
        self.semh = {}
        for name in ("spe", "sact", "sdve", "sgp", "dsync", "dgp"):
            self.semh[name] = nc.semaphore(name).__enter__()

    def _sem_for(self, eng, fn):
        if fn in ("dma_start", "dma_start_transpose"):
            return ("dgp", 16) if eng == "gpsimd" else ("dsync", 16)
        return ({"tensor": "spe", "scalar": "sact", "vector": "sdve",
                 "gpsimd": "sgp", "sync": "dsync"}[eng], 1)

    def do(self, eng, fn, *args, rd=(), wr=(), **kwargs):
        sem, amt = self._sem_for(eng, fn)
        op = Op(eng, fn, args, kwargs, sem, amt)
        if amt == 16:
            op.need_inc = True
        deps = set()
        for r in rd:
            d = self.lastw.get(r)
            if d is not None:
                deps.add(d)
        for w in wr:
            d = self.lastw.get(w)
            if d is not None:
                deps.add(d)
            for d2 in self.readers.get(w, ()):
                deps.add(d2)
        my_idx = len(self.ops[eng])
        op.idx = my_idx
        for d in deps:
            if d is op:
                continue
            if d.eng != eng:
                d.need_inc = True
                op.waits.add(d)
            else:
                dist = my_idx - d.idx
                slow = d.fn in ("reduce_sum", "tensor_reduce", "reciprocal")
                if dist <= (2 if slow else 1) and not d.raw:
                    d.need_inc = True
                    op.waits.add(d)
        self.ops[eng].append(op)
        for w in wr:
            self.lastw[w] = op
            self.readers[w] = []
        for r in rd:
            self.readers.setdefault(r, []).append(op)
        return op

    def barrier_wait(self, eng, target_op):
        target_op.need_inc = True
        op = Op(eng, None, None, None, None, None, raw=True)
        op.waits.add(target_op)
        self.ops[eng].append(op)
        return op

    def finalize_and_emit(self):
        for eng in ENGINES:
            for op in self.ops[eng]:
                if not op.waits:
                    continue
                best = {}
                for d in op.waits:
                    k = (d.eng, d.sem)
                    if k not in best or d.idx > best[k].idx:
                        best[k] = d
                op.waits = set(best.values())
        for eng in ENGINES:
            for op in self.ops[eng]:
                op.need_inc = (op.amt == 16) if not op.raw else False
        for eng in ENGINES:
            for op in self.ops[eng]:
                for d in op.waits:
                    d.need_inc = True
        counts = {}
        for eng in ENGINES:
            for op in self.ops[eng]:
                if op.raw:
                    continue
                if op.need_inc:
                    counts[op.sem] = counts.get(op.sem, 0) + op.amt
                    op.val = counts[op.sem]
        nc = self.nc
        semh = self.semh

        def emitter(eng):
            def run(e):
                water = {}
                for op in self.ops[eng]:
                    for d in sorted(op.waits, key=lambda o: (o.sem, o.val)):
                        assert d.val is not None, f"dep without inc: {d.eng}.{d.fn}"
                        if water.get(d.sem, 0) < d.val:
                            e.wait_ge(semh[d.sem], d.val)
                            water[d.sem] = d.val
                    if op.raw:
                        continue
                    ins = getattr(e, op.fn)(*op.args, **op.kwargs)
                    if op.need_inc:
                        ins.then_inc(semh[op.sem], op.amt)
            return run

        with nc.Block() as block:
            for eng in ENGINES:
                getattr(block, eng)(emitter(eng))


def ts(i, n):
    return slice(i * n, (i + 1) * n)


def build_program():
    nc = bass.Bass("TRN2", target_bir_lowering=False)

    d_x = nc.dram_tensor("x", [T, C], F32, kind="ExternalInput").ap()
    d_out = nc.dram_tensor("out", [T, C], F32, kind="ExternalOutput").ap()
    dw = {}
    for name, shape in [
        ("wcd", [C, LD]), ("wqup", [LD, C]), ("wkv", [LD, 2 * HD]), ("wcp", [C, C]),
        ("wq", [C, C]), ("wk", [C, C]), ("wv", [C, C]), ("wo", [C, C]),
        ("bq", [C]), ("bk", [C]), ("bv", [C]), ("bo", [C]),
        ("lng", [C]), ("lnb", [C]),
        ("fw", [C, C]), ("falpha", [1]), ("fbias", [C]),
    ]:
        dw[name] = nc.dram_tensor(name, shape, F32, kind="ExternalInput").ap()
    d_dbgb = nc.dram_tensor("dbgb", [1, 8192], BF16, kind="ExternalOutput").ap()
    d_dbgw = nc.dram_tensor("dbgw", [128, 1024], BF16, kind="ExternalOutput").ap()
    d_dbgo = nc.dram_tensor("dbgo", [128, 8, 512], BF16, kind="ExternalOutput").ap()
    d_dbgs = nc.dram_tensor("dbgs", [128, 512], BF16, kind="ExternalOutput").ap()
    d_dbga = nc.dram_tensor("dbga", [128, 9], F32, kind="ExternalOutput").ap()
    d_dbgf = nc.dram_tensor("dbgf", [128, 16], F32, kind="ExternalOutput").ap()
    d_cbf = nc.dram_tensor("cbf", [128, 896 + 128 + 128], BF16, kind="ExternalInput").ap()
    d_cf32 = nc.dram_tensor("cf32", [128, 128], F32, kind="ExternalInput").ap()

    A = nc.alloc_sbuf_tensor
    XT = A("XT", [128, NCH, SEG], F32)
    XB = A("XB", [128, NCH, SEG], BF16)
    KDUP = A("KDUP", [128, T], BF16)
    VEXT = A("VEXT", [128, T // 128, 66], BF16)
    WCD = A("WCD", [128, NCH, LD], BF16)
    WQUP = A("WQUP", [128, LD // 128, C], BF16)
    WKV = A("WKV", [128, LD // 128, 2 * HD], BF16)
    WCP = A("WCP", [128, NCH, C], BF16)
    WFT = A("WFT", [128, NCH, C], BF16)
    WQT = A("WQT", [128, NCH, C], BF16)
    WKT = A("WKT", [128, NCH, C], BF16)
    WVT = A("WVT", [128, NCH, C], BF16)
    WGT = A("WGT", [128, NCH, C], BF16)
    MBIG = A("MBIG", [128, 896], BF16)
    TRIU = A("TRIU", [128, 128], BF16)
    IDBF = A("IDBF", [128, 128], BF16)
    IDF = A("IDF", [128, 128], F32)
    ONESC = A("ONESC", [128, 1], BF16)
    ONESCF = A("ONESCF", [128, 1], F32)
    CEXP = A("CEXP", [128, 2], F32)
    CNH = A("CNH", [128, 1], F32)
    CEPS = A("CEPS", [128, 1], F32)
    BQC = A("BQC", [128, NCH], F32)
    BKC = A("BKC", [128, NCH], F32)
    LNGC = A("LNGC", [128, NCH], F32)
    LNBCB = A("LNBCB", [128, NCH], BF16)
    FBC = A("FBC", [128, NCH], F32)
    BWOC = A("BWOC", [128, NCH], F32)
    DBGS = A("DBGS", [128, 10], F32)
    DELC = A("DELC", [128, 1], F32)
    NDELC = A("NDELC", [128, 1], F32)
    ALFC = A("ALFC", [128, 1], F32)
    ROWB = A("ROWB", [1, 8192], BF16)
    ROWF = A("ROWF", [1, 1280], F32)
    ROWD = A("ROWD", [1, 1024], F32)
    RSB = A("RSB", [128, SEG], F32)
    RST = A("RST", [128, NTT], F32)
    QTB = A("QTB", [128, NCH, SEG], BF16)
    KTB = A("KTB", [128, NCH, SEG], BF16)
    KNB = A("KNB", [128, 2, C], BF16)
    VNB = A("VNB", [128, 2, C], BF16)
    ZT = A("ZT", [128, 2, SEG], BF16)
    ET = A("ET", [128, 1, SEG], BF16)
    RT = A("RT", [128, 1, SEG], BF16)
    SQ = QTB

    def _addr(h):
        return nc.lookup_mloc(h).addr

    aq, ak = _addr(QTB), _addr(KTB)
    akn, avn = _addr(KNB), _addr(VNB)
    az, ae, ar = _addr(ZT), _addr(ET), _addr(RT)
    assert ak == aq + 8192 and avn == akn + 4096 and ae == az + 2048 and ar == ae + 1024
    AT = nc.alloc_sbuf_tensor_at
    LATB = AT("LATB", [128, LD // 128, SEG], BF16, offset=aq)
    QH = AT("QH", [128, 2, SEG], BF16, offset=aq + 4096)
    EXPB = AT("EXPB", [128, 4, 1024], BF16, offset=ak)
    XIN = AT("XIN", [128, 2, 1024], F32, offset=ak)
    WSTG = AT("WSTG", [128, 4, C], F32, offset=aq)
    WSTGB = AT("WSTGB", [128, 4, C], BF16, offset=akn)
    YT = AT("YT", [128, NCH, SEG], BF16, offset=akn)
    XON = AT("XON", [128, 2, C], F32, offset=akn)
    YN = AT("YN", [128, 4, 66], BF16, offset=az)
    DDIA = AT("DDIA", [128, 4, 128], BF16, offset=az + 544)
    RRC = AT("RRC", [128, 4], F32, offset=az + 1568)
    BM = AT("BM", [128, 2, 4, 128], BF16, offset=az)
    SSB = AT("SSB", [128, 512], BF16, offset=az + 2048)
    T1F = AT("T1F", [128, 2, SEG], F32, offset=az)
    ABSA = AT("ABSA", [128, NCH + 1], F32, offset=_addr(RSB))
    WSTGF0 = AT("WSTGF0", [128, 4, C], F32, offset=_addr(XT))
    WSTGF1 = AT("WSTGF1", [128, 4, C], F32, offset=_addr(WQT))
    WSTGB_A = AT("WSTGB_A", [128, 4, C], BF16, offset=aq)
    WSTGB_XB = AT("WSTGB_XB", [128, 4, C], BF16, offset=_addr(XB))
    WSTGB_KD = AT("WSTGB_KD", [128, 4, C], BF16, offset=_addr(KDUP))
    WSTGB_G1 = AT("WSTGB_G1", [128, 4, C], BF16, offset=_addr(WGT))
    WSTGB_G2 = AT("WSTGB_G2", [128, 4, C], BF16, offset=_addr(WGT) + 8192)

    rem = nc.sbuf_bytes_remaining
    assert rem >= 0, f"SBUF over budget by {-rem} bytes"

    PTAB = nc.alloc_psum_tensor("PTAB", [128, 2048], F32)
    PA = PTAB[:, 0:1024]
    PB = PTAB[:, 1024:2048]
    PC = nc.alloc_psum_tensor("PC", [128, 512], F32)
    PD = nc.alloc_psum_tensor("PD", [128, 512], F32)
    PBF = nc.alloc_psum_tensor("PBF", [128, 1024], BF16)
    SPS = nc.alloc_psum_tensor("SPS", [128, 512], F32)

    p = Prog(nc)
    rPA0, rPA1, rPB0, rPB1, rPC, rPD, rPBF, rSPS = "PT0", "PT1", "PT2", "PT3", "PC", "PD", "PBF", "SPS"

    def _flat(t):
        return t[:].rearrange("p a f -> p (a f)")

    XBf = _flat(XB)
    WCDf = _flat(WCD)
    WQUPf = _flat(WQUP)
    WKVf = _flat(WKV)
    WCPf = _flat(WCP)
    WFTf = _flat(WFT)
    WVTf = _flat(WVT)
    WGTf = _flat(WGT)
    LATBf = _flat(LATB)
    QTBf = _flat(QTB)
    KTBf = _flat(KTB)
    KNBf = _flat(KNB)
    VNBf = _flat(VNB)
    YTf = _flat(YT)

    R1 = ["QTB", "KTB", "LATB", "QH0", "QH1", "EXPB0", "EXPB1", "EXPB2", "EXPB3",
          "XIN0", "XIN1", "WSTG", "WSTGB_A", "XONQ"]
    R2 = ["KNB0", "KNB1", "VNB0", "VNB1", "YT", "XONT0", "XONT1", "WSTGB"]
    R6 = ["XB", "OB", "WSTGB_XB"]
    R7 = ["KDUP", "WSTGB_KD"]
    R8 = ["WGT", "WSTGB_G1", "WSTGB_G2"]
    R9 = ["WQT", "WSTGF1"]
    R3 = ["ZT0", "ZT1", "ET", "RT", "BM0", "BM1", "SSB", "T1F0", "T1F1",
          "YN0", "YN1", "YN2", "YN3", "DDIA0", "DDIA1", "DDIA2", "DDIA3",
          "RRC0", "RRC1", "RRC2", "RRC3"]
    R4 = ["RSB", "ABSA"]
    REGION = {}
    for lst in (R1, R2, R3, R4, R6, R7, R8, R9):
        for n in lst:
            REGION[n] = lst

    def claim(*names):
        for new in names:
            lst = p.readers.setdefault(new, [])
            for o in REGION.get(new, ()):
                if o == new:
                    continue
                if o in p.lastw:
                    lst.append(p.lastw[o])
                lst.extend(p.readers.get(o, ()))

    _off = {"b": 0, "f": 0, "d": 0}

    def _row(kind, width, cap):
        off = _off[kind]
        _off[kind] = off + width
        assert _off[kind] <= cap, kind
        return (off, width)

    def rb(slot, lo=0, n=None):
        off, w = slot
        n = w if n is None else n
        return ROWB[0:1, off + lo: off + lo + n]

    def rf(slot, lo=0, n=None):
        off, w = slot
        n = w if n is None else n
        return ROWF[0:1, off + lo: off + lo + n]

    def rd_(slot, lo=0, n=None):
        off, w = slot
        n = w if n is None else n
        return ROWD[0:1, off + lo: off + lo + n]

    ones_row = _row("b", 512, 8192)
    cs_wcd = _row("b", 512, 8192)
    cs_wft = _row("b", 1024, 8192)
    cs_wq = _row("b", 1024, 8192)
    cs_wk = _row("b", 1024, 8192)
    cs_wv = _row("b", 1024, 8192)
    cs_wg = _row("b", 1024, 8192)
    bv_row = _row("b", 1024, 8192)
    negmrs_row = _row("b", 512, 8192)
    invrs_row = _row("b", 512, 8192)
    onesf_row = _row("f", 128, 1280)
    rs_row = _row("f", 512, 1280)
    negmf_row = _row("f", 512, 1280)
    drow_s = _row("f", 1, 1280)
    arow_s = _row("f", 1, 1280)
    m_row = _row("d", 512, 1024)
    s_row = _row("d", 512, 1024)

    def startup():
        p.do("sync", "dma_start", out=MBIG[:], in_=d_cbf[:, 0:896], wr=("MBIG",))
        p.do("sync", "dma_start", out=TRIU[:], in_=d_cbf[:, 896:1024], wr=("TRIU",))
        p.do("sync", "dma_start", out=IDBF[:], in_=d_cbf[:, 1024:1152], wr=("IDBF",))
        p.do("sync", "dma_start", out=IDF[:], in_=d_cf32[:], wr=("IDF",))
        p.do("vector", "memset", ONESC[:], 1.0, wr=("ONESC",))
        p.do("vector", "memset", ONESCF[:], 1.0, wr=("ONESCF",))
        p.do("vector", "memset", CEXP[:, 0:1], -CSH, wr=("CEXP",))
        p.do("vector", "memset", CEXP[:, 1:2], 0.125, wr=("CEXP",))
        p.do("vector", "memset", CNH[:], -0.5, wr=("CNH",))
        p.do("vector", "memset", CEPS[:], EPS, wr=("CEPS",))
        p.do("vector", "memset", rb(ones_row), 1.0, wr=("ROWB",))
        p.do("vector", "memset", rf(onesf_row), 1.0, wr=("ROWF",))
        p.do("vector", "memset", VEXT[:, :, 64:65], 1.0, wr=("VEXT_ones",))
        p.do("vector", "memset", VEXT[:, :, 65:66], 0.0, wr=("VEXT_ones",))

        p.do("gpsimd", "dma_start", out=WCD[:],
             in_=dw["wcd"].rearrange("(a p) f -> p a f", p=128), wr=("WCD",))
        p.do("gpsimd", "dma_start", out=WQUP[:],
             in_=dw["wqup"].rearrange("(a p) f -> p a f", p=128), wr=("WQUP",))
        p.do("gpsimd", "dma_start", out=WKV[:],
             in_=dw["wkv"].rearrange("(a p) f -> p a f", p=128), wr=("WKV",))
        p.do("gpsimd", "dma_start", out=WCP[:],
             in_=dw["wcp"].rearrange("(a p) f -> p a f", p=128), wr=("WCP",))
        load_tile(0, 0)
        load_tile(0, 1)
        p.do("gpsimd", "dma_start", out=WSTGF0[:],
             in_=dw["fw"][0:512, :].rearrange("(a p) f -> p a f", p=128), wr=("WSTGF0", "XT"))
        p.do("gpsimd", "dma_start", out=WSTGF1[:],
             in_=dw["fw"][512:1024, :].rearrange("(a p) f -> p a f", p=128), wr=("WSTGF1", "WQT"))
        for stg, rstg, srcn, half in [(WSTGB_A, "WSTGB_A", "wq", 0), (WSTGB_XB, "WSTGB_XB", "wq", 1),
                                      (WSTGB_KD, "WSTGB_KD", "wk", 0), (WSTGB_G1, "WSTGB_G1", "wk", 1),
                                      (WSTGB_G2, "WSTGB_G2", "wv", 0)]:
            p.do("gpsimd", "dma_start", out=stg[:],
                 in_=dw[srcn][half * 512:(half + 1) * 512, :].rearrange("(a p) f -> p a f", p=128),
                 wr=(rstg,))
        p.do("gpsimd", "dma_start", out=rb(bv_row),
             in_=dw["bv"].rearrange("(a f) -> a f", a=1), wr=("ROWB",))
        p.do("gpsimd", "dma_start", out=rf(arow_s),
             in_=dw["falpha"].rearrange("(a f) -> a f", a=1), wr=("arow",))

        p.do("vector", "memset", ABSA[:], 0.0, wr=("ABSA",))
        for a in range(4):
            p.do("vector", "reduce_sum", ABSA[:, a:a + 1], WSTGF0[:, a, :],
                 axis=AX.X, apply_absolute_value=True, rd=("WSTGF0", "XT"), wr=("ABSA",))
        for a in range(4):
            p.do("vector", "reduce_sum", ABSA[:, 4 + a:4 + a + 1], WSTGF1[:, a, :],
                 axis=AX.X, apply_absolute_value=True, rd=("WSTGF1", "WQT"), wr=("ABSA",))
        p.do("vector", "memset", DBGS[:, 9:10], 0.0, wr=("PAD",))
        p.do("vector", "memset", DBGS[:, 9:10], 1.0, wr=("PAD",))
        p.do("vector", "reduce_sum", ABSA[:, 8:9], ABSA[:, 0:8], axis=AX.X, rd=("ABSA",), wr=("ABSA1",))
        p.do("vector", "memset", DBGS[:, 9:10], 0.0, wr=("PAD",))
        p.do("vector", "memset", DBGS[:, 9:10], 1.0, wr=("PAD",))
        p.do("vector", "tensor_copy", DBGS[:, 0:9], ABSA[:], rd=("ABSA", "ABSA1"), wr=("DBGS",))
        p.do("tensor", "matmul", PC[0:1, 0:1], ONESCF[:], ABSA[:, 8:9], start=True, stop=True,
             rd=("ABSA1", "ONESCF"), wr=(rPC,))
        p.do("vector", "tensor_scalar_mul", rf(drow_s), PC[0:1, 0:1], 0.7 / (C * C),
             rd=(rPC,), wr=("drow",))
        p.do("tensor", "matmul", PD[:, 0:1], rf(onesf_row), rf(drow_s), start=True, stop=True,
             rd=("drow", "ROWF"), wr=(rPD,))
        p.do("vector", "tensor_copy", DELC[:], PD[:, 0:1], rd=(rPD,), wr=("DELC",))
        p.do("vector", "tensor_scalar_mul", NDELC[:], DELC[:], -1.0, rd=("DELC",), wr=("NDELC",))
        p.do("tensor", "matmul", PD[:, 1:2], rf(onesf_row), rf(arow_s), start=True, stop=True,
             rd=("arow", "ROWF"), wr=(rPD,))
        p.do("vector", "tensor_copy", ALFC[:], PD[:, 1:2], rd=(rPD,), wr=("ALFC",))

        def wtrans(stg, rstg, dst, rdst, half):
            for ci in range(NCH):
                for cj in range(4):
                    p.do("tensor", "transpose", PBF[:, ts(cj, 128)], stg[:, cj, ts(ci, 128)],
                         IDBF[:], rd=(rstg, "IDBF"), wr=(rPBF,))
                p.do("vector", "tensor_copy", dst[:, ci, ts(half, 512)], PBF[:, 0:512],
                     rd=(rPBF,), wr=(rdst,))
        wtrans(WSTGB_KD, "WSTGB_KD", WKT, "WKT", 0)
        wtrans(WSTGB_G1, "WSTGB_G1", WKT, "WKT", 1)
        wtrans(WSTGB_G2, "WSTGB_G2", WVT, "WVT", 0)

        claim("WSTGB")
        for half in range(2):
            SRC, rsrc_, rx = (WSTGF0, "WSTGF0", "XT") if half == 0 else (WSTGF1, "WSTGF1", "WQT")
            for a in range(4):
                for hb in range(2):
                    sl = slice(hb * 512, (hb + 1) * 512)
                    p.do("vector", "tensor_scalar", WSTGB[:, a, sl], SRC[:, a, sl], DELC[:], None,
                         ALU.is_gt, rd=(rsrc_, rx, "DELC"), wr=("c1",))
                    p.do("vector", "tensor_scalar", ZT[:, 0, :], SRC[:, a, sl], NDELC[:], None,
                         ALU.is_lt, rd=(rsrc_, rx, "NDELC"), wr=("c2",))
                    p.do("vector", "tensor_sub", WSTGB[:, a, sl], WSTGB[:, a, sl], ZT[:, 0, :],
                         rd=("c1", "c2"), wr=("c1",))
                    p.do("vector", "tensor_scalar_mul", WSTGB[:, a, sl], WSTGB[:, a, sl], ALFC[:],
                         rd=("c1", "ALFC"), wr=("WSTGB",))
            for ci in range(NCH):
                for cj in range(4):
                    p.do("tensor", "transpose", PBF[:, ts(cj, 128)], WSTGB[:, cj, ts(ci, 128)], IDBF[:],
                         rd=("WSTGB", "IDBF"), wr=(rPBF,))
                p.do("vector", "tensor_copy", WFT[:, ci, ts(half, 512)], PBF[:, 0:512],
                     rd=(rPBF,), wr=("WFT",))

        wtrans(WSTGB_A, "WSTGB_A", WQT, "WQT", 0)
        wtrans(WSTGB_XB, "WSTGB_XB", WQT, "WQT", 1)
        claim("WSTGB_G1")
        p.do("gpsimd", "dma_start", out=WSTGB_G1[:],
             in_=dw["wv"][512:1024, :].rearrange("(a p) f -> p a f", p=128), wr=("WSTGB_G1",))
        claim("WSTGB_XB")
        p.do("gpsimd", "dma_start", out=WSTGB_XB[:],
             in_=dw["wo"][0:512, :].rearrange("(a p) f -> p a f", p=128), wr=("WSTGB_XB",))
        claim("WSTGB_KD")
        p.do("gpsimd", "dma_start", out=WSTGB_KD[:],
             in_=dw["wo"][512:1024, :].rearrange("(a p) f -> p a f", p=128), wr=("WSTGB_KD",))
        wtrans(WSTGB_G1, "WSTGB_G1", WVT, "WVT", 1)
        claim("WGT")
        wtrans(WSTGB_XB, "WSTGB_XB", WGT, "WGT", 0)
        wtrans(WSTGB_KD, "WSTGB_KD", WGT, "WGT", 1)

        STG = XT[0:1].rearrange("p a f -> p (a f)")
        for srcn, dstc, rdst in [("bq", BQC, "BQC"), ("bk", BKC, "BKC"), ("lng", LNGC, "LNGC"),
                                 ("fbias", FBC, "FBC")]:
            p.do("sync", "dma_start", out=STG[:, 0:1024],
                 in_=dw[srcn].rearrange("(a f) -> a f", a=1), wr=("STGROW", "XT"))
            for ci in range(NCH):
                p.do("tensor", "transpose", PC[:, ci:ci + 1], STG[:, ts(ci, 128)], IDF[0:1, 0:1],
                     rd=("STGROW", "IDF"), wr=(rPC,))
            p.do("vector", "tensor_copy", dstc[:], PC[:, 0:NCH], rd=(rPC,), wr=(rdst,))
        p.do("sync", "dma_start", out=STG[:, 0:1024], in_=dw["lnb"].rearrange("(a f) -> a f", a=1),
             wr=("STGROW", "XT"))
        for ci in range(NCH):
            p.do("tensor", "transpose", PC[:, ci:ci + 1], STG[:, ts(ci, 128)], IDF[0:1, 0:1],
                 rd=("STGROW", "IDF"), wr=(rPC,))
        p.do("vector", "tensor_copy", LNBCB[:], PC[:, 0:NCH], rd=(rPC,), wr=("LNBCB",))

        p.do("sync", "dma_start", out=STG[:, 2048:3072], in_=dw["bo"].rearrange("(a f) -> a f", a=1),
             wr=("STGBO", "XT"))
        for half in range(2):
            for ci in range(NCH):
                p.do("tensor", "matmul", PD[0:1, 0:512], LNBCB[:, ci:ci + 1],
                     WGT[:, ci, ts(half, 512)], start=(ci == 0), stop=(ci == NCH - 1),
                     rd=("LNBCB", "WGT"), wr=(rPD,))
            p.do("vector", "tensor_tensor", STG[:, 1024 + 512 * half: 1024 + 512 * (half + 1)],
                 PD[0:1, 0:512], STG[:, 2048 + 512 * half: 2048 + 512 * (half + 1)], ALU.add,
                 rd=(rPD, "STGBO"), wr=("STGBW", "XT"))
        for ci in range(NCH):
            p.do("tensor", "transpose", PC[:, ci:ci + 1], STG[:, 1024 + 128 * ci: 1152 + 128 * ci], IDF[0:1, 0:1],
                 rd=("STGBW", "IDF"), wr=(rPC,))
        p.do("vector", "tensor_copy", BWOC[:], PC[:, 0:NCH], rd=(rPC,), wr=("BWOC",))
        for ci in range(NCH):
            p.do("vector", "tensor_scalar_mul", WGT[:, ci, :], WGT[:, ci, :], LNGC[:, ci:ci + 1],
                 rd=("WGT", "LNGC"), wr=("WGT",))

        def colsum_rows(wt, rdw, slot, width):
            for half in range(width // 512):
                for ci in range(NCH):
                    p.do("tensor", "matmul", PC[0:1, 0:512], ONESC[:], wt[:, ci, ts(half, 512)],
                         start=(ci == 0), stop=(ci == NCH - 1), rd=(rdw, "ONESC"), wr=(rPC,))
                p.do("vector", "tensor_copy", rb(slot, half * 512, 512), PC[0:1, 0:512],
                     rd=(rPC,), wr=("ROWB",))
        colsum_rows(WCD, "WCD", cs_wcd, 512)
        colsum_rows(WFT, "WFT", cs_wft, 1024)
        colsum_rows(WQT, "WQT", cs_wq, 1024)
        colsum_rows(WKT, "WKT", cs_wk, 1024)
        colsum_rows(WVT, "WVT", cs_wv, 1024)
        colsum_rows(WGT, "WGT", cs_wg, 1024)

    def load_tile(g, tt):
        buf = tt % 2
        claim(f"XIN{buf}")
        rows = d_x[g * SEG + tt * 128: g * SEG + (tt + 1) * 128, :]
        p.do("gpsimd", "dma_start", out=XIN[:, buf, :], in_=rows, wr=(f"XIN{buf}",))

    def trans_in(g):
        for tt in range(NTT):
            buf = tt % 2
            bank, rb_ = (PC, rPC) if tt % 2 == 0 else (PD, rPD)
            for c4 in range(2):
                for ci in range(4 * c4, 4 * c4 + 4):
                    p.do("tensor", "transpose", bank[:, ts(ci % 4, 128)], XIN[:, buf, ts(ci, 128)], IDF[:],
                         rd=(f"XIN{buf}", "IDF"), wr=(rb_,))
                p.do("vector", "tensor_copy",
                     XT[:, 4 * c4:4 * c4 + 4, ts(tt, 128)], bank[:, 0:512].rearrange("p (a f) -> p a f", a=4),
                     rd=(rb_,), wr=("XT",))
            if tt + 2 < NTT:
                load_tile(g, tt + 2)

    def cast_xb():
        claim("XB")
        for ci in range(NCH):
            p.do("vector", "tensor_copy", XB[:, ci, :], XT[:, ci, :], rd=("XT",), wr=("XB",))

    def ln_stats(src, rsrc, extra=False):
        if rsrc == "XB":
            claim("QTB")
        for ci in range(NCH):
            p.do("vector", "tensor_mul", SQ[:, ci, :], src[:, ci, :], src[:, ci, :],
                 rd=(rsrc,), wr=("QTB",))
        srcf = src[:].rearrange("p a f -> p (a f)")
        SQf = SQ[:].rearrange("p a f -> p (a f)")
        for ci in range(NCH):
            p.do("tensor", "matmul", PA[0:1, 0:512], ONESC[:], srcf[:, ci * SEG:(ci + 1) * SEG],
                 start=(ci == 0), stop=(ci == NCH - 1), rd=(rsrc, "ONESC"), wr=(rPA0,))
        for ci in range(NCH):
            p.do("tensor", "matmul", PA[0:1, 512:1024], ONESC[:], SQf[:, ci * SEG:(ci + 1) * SEG],
                 start=(ci == 0), stop=(ci == NCH - 1), rd=("QTB", "ONESC"), wr=(rPA1,))
        p.do("vector", "tensor_scalar_mul", rd_(m_row), PA[0:1, 0:512], 1.0 / C, rd=(rPA0,), wr=("m_row",))
        p.do("vector", "tensor_mul", rd_(s_row), rd_(m_row), rd_(m_row), rd=("m_row",), wr=("s_row",))
        p.do("vector", "scalar_tensor_tensor", rd_(s_row), PA[0:1, 512:1024], 1.0 / C, rd_(s_row),
             ALU.mult, ALU.subtract, rd=(rPA1, "s_row"), wr=("s_row",))
        p.do("scalar", "activation", rd_(s_row), rd_(s_row), AF.Ln, bias=CEPS[0:1, 0:1],
             rd=("s_row", "CEPS"), wr=("s_row",))
        p.do("scalar", "activation", rf(rs_row), rd_(s_row), AF.Exp, scale=CNH[0:1, 0:1],
             rd=("s_row", "CNH"), wr=("rs_row",))
        if extra:
            p.do("vector", "reciprocal", rd_(s_row), rf(rs_row), rd=("rs_row",), wr=("s_row",))
        p.do("vector", "tensor_scalar_mul", rb(negmrs_row), rd_(m_row), -1.0, rd=("m_row",), wr=("negmrs_row",))
        if extra:
            p.do("vector", "tensor_scalar_mul", rf(negmf_row), rd_(m_row), -1.0, rd=("m_row",), wr=("negmf_row",))
        p.do("tensor", "matmul", PB[:, 0:512], rf(onesf_row), rf(rs_row), start=True, stop=True,
             rd=("rs_row", "ROWF"), wr=(rPB0,))
        p.do("vector", "tensor_copy", RSB[:], PB[:, 0:512], rd=(rPB0,), wr=("RSB",))
        if extra:
            p.do("vector", "tensor_copy", rb(invrs_row), rd_(s_row), rd=("s_row",), wr=("invrs_row",))

    def rs_transposed():
        for tt in range(NTT):
            p.do("tensor", "transpose", PC[:, tt:tt + 1], rf(rs_row, tt * 128, 128), IDF[0:1, 0:1],
                 rd=("rs_row", "IDF"), wr=(rPC,))
        p.do("vector", "tensor_copy", RST[:], PC[:, 0:NTT], rd=(rPC,), wr=("RST",))

    def sub1(g):
        cast_xb()
        ln_stats(XB, "XB")
        claim("LATB", "QH0", "QH1")
        for lp in range(LD // 256):
            lm0, lm1 = 2 * lp, 2 * lp + 1
            for ci in range(NCH):
                p.do("tensor", "matmul", PC[:, 0:512], WCDf[:, ci * LD + lm0 * 128: ci * LD + (lm0 + 1) * 128],
                     XBf[:, ci * SEG:(ci + 1) * SEG],
                     start=(ci == 0), stop=False, rd=("WCD", "XB"), wr=(rPC,))
                p.do("tensor", "matmul", PD[:, 0:512], WCDf[:, ci * LD + lm1 * 128: ci * LD + (lm1 + 1) * 128],
                     XBf[:, ci * SEG:(ci + 1) * SEG],
                     start=(ci == 0), stop=False, rd=("WCD", "XB"), wr=(rPD,))
            p.do("tensor", "matmul", PC[:, 0:512], rb(cs_wcd, lm0 * 128, 128), rb(negmrs_row),
                 start=False, stop=True, rd=("ROWB", "negmrs_row"), wr=(rPC,))
            p.do("tensor", "matmul", PD[:, 0:512], rb(cs_wcd, lm1 * 128, 128), rb(negmrs_row),
                 start=False, stop=True, rd=("ROWB", "negmrs_row"), wr=(rPD,))
            p.do("vector", "tensor_tensor", LATB[:, lm0, :], PC[:, 0:512], RSB[:], ALU.mult,
                 rd=(rPC, "RSB"), wr=("LATB",))
            p.do("vector", "tensor_tensor", LATB[:, lm1, :], PD[:, 0:512], RSB[:], ALU.mult,
                 rd=(rPD, "RSB"), wr=("LATB",))
        for dup in range(2):
            for lc in range(LD // 128):
                p.do("tensor", "matmul", PC[64 * dup:64 * dup + 64, 0:512],
                     WKVf[:, lc * 128: lc * 128 + 64], LATBf[:, lc * SEG:(lc + 1) * SEG],
                     start=(lc == 0), stop=(lc == LD // 128 - 1), rd=("WKV", "LATB"), wr=(rPC,))
        if g == 0:
            claim("KDUP")
        p.do("vector", "tensor_copy", KDUP[:, g * SEG:(g + 1) * SEG], PC[:, 0:512], rd=(rPC,), wr=("KDUP",))
        for tt in range(NTT):
            for lc in range(LD // 128):
                p.do("tensor", "matmul", PD[:, ts(tt, 64)],
                     LATBf[:, lc * SEG + tt * 128: lc * SEG + (tt + 1) * 128],
                     WKVf[:, lc * 128 + 64: lc * 128 + 128],
                     start=(lc == 0), stop=(lc == LD // 128 - 1), rd=("WKV", "LATB"), wr=(rPD,))
        p.do("vector", "tensor_copy", VEXT[:, 4 * g:4 * g + 4, 0:64],
             PD[:, 0:256].rearrange("p (a f) -> p a f", a=4), rd=(rPD,), wr=("VEXT",))
        imax = 4 * g + 3
        PQ = PBF[:].bitcast(F32)

        def qgen(pr2):
            qb2 = pr2 % 2
            for lc in range(LD // 128):
                p.do("tensor", "matmul", PQ[:, 0:512],
                     WQUPf[:, lc * C + pr2 * 128: lc * C + (pr2 + 1) * 128],
                     LATBf[:, lc * SEG:(lc + 1) * SEG],
                     start=(lc == 0), stop=(lc == LD // 128 - 1), rd=("WQUP", "LATB"), wr=(rPBF,))
            p.do("vector", "tensor_copy", QH[:, qb2, :], PQ[:, 0:512], rd=(rPBF,), wr=(f"QH{qb2}",))

        def emit_qk(pr2, i):
            qb2 = pr2 % 2
            qe = 2 * (i % 2)
            qo = qe + 1
            p.do("tensor", "matmul", PTAB[:, qe * 512:(qe + 1) * 512],
                 KDUP[0:64, ts(i, 128)], QH[0:64, qb2, :],
                 start=True, stop=True, tile_position=(0, 0), rd=("KDUP", f"QH{qb2}"), wr=(f"PT{qe}",))
            p.do("tensor", "matmul", PTAB[:, qo * 512:(qo + 1) * 512],
                 KDUP[64:128, ts(i, 128)], QH[64:128, qb2, :],
                 start=True, stop=True, tile_position=(64, 0), rd=("KDUP", f"QH{qb2}"), wr=(f"PT{qo}",))

        qk_pre = [False] * (NH // 2)
        for pr in range(NH // 2):
            if pr == 0:
                claim("EXPB0", "EXPB1", "EXPB2", "EXPB3", "YT")
                for tt in range(NTT):
                    claim(f"YN{tt}", f"DDIA{tt}", f"RRC{tt}")
                qgen(0)
            qbuf = pr % 2
            if not qk_pre[pr]:
                for _pre in range(min(2, imax + 1)):
                    emit_qk(pr, _pre)
            qgen_done = False
            i = 0
            while i <= imax:
                ni = 2 if i + 1 <= imax else 1
                for ii in range(i, i + ni):
                    ebx = ii % 4
                    qlo = 2 * (ii % 2)
                    p.do("scalar", "activation", EXPB[:, ebx, :],
                         PTAB[:, qlo * 512: qlo * 512 + 1024], AF.Exp,
                         bias=CEXP[:, 0:1], scale=CEXP[:, 1:2],
                         rd=(f"PT{qlo}", f"PT{qlo + 1}", "CEXP"), wr=(f"EXPB{ebx}",))
                for ii in range(i, i + ni):
                    ebb = ii % 4
                    r = ii - 4 * g
                    if r >= 0:
                        moff = 384 - 128 * r
                        msk = MBIG[:, moff:moff + 512].rearrange("p (a f) -> p a f", a=1).broadcast_to([128, 2, 512])
                        p.do("vector", "tensor_tensor", EXPB[:, ebb, :].rearrange("p (a f) -> p a f", a=2),
                             EXPB[:, ebb, :].rearrange("p (a f) -> p a f", a=2), msk, ALU.mult,
                             rd=(f"EXPB{ebb}", "MBIG"), wr=(f"EXPB{ebb}",))
                for ii in range(i + 2, min(i + 2 + ni, imax + 1)):
                    emit_qk(pr, ii)
                if not qgen_done and i >= max(0, imax // 2 - 1) and pr + 1 < NH // 2:
                    qgen(pr + 1)
                    qgen_done = True
                EXPF = EXPB[:].rearrange("p a f -> p (a f)")
                VEXF = VEXT[:].rearrange("p a f -> p (a f)")
                for ii in range(i, i + ni):
                    ebb = ii % 4
                    for tt in range(NTT):
                        if ii > 4 * g + tt:
                            continue
                        for hh in range(2):
                            ybank, ry = (PC, rPC) if hh == 0 else (PD, rPD)
                            p.do("tensor", "matmul", ybank[:, tt * 66:(tt + 1) * 66],
                                 EXPF[:, ebb * 1024 + hh * 512 + tt * 128: ebb * 1024 + hh * 512 + (tt + 1) * 128],
                                 VEXF[:, ii * 66:(ii + 1) * 66],
                                 start=(ii == 0 and tt == 0), stop=(ii == imax and tt == NTT - 1),
                                 skip_group_check=True,
                                 rd=(f"EXPB{ebb}", "VEXT", "VEXT_ones"), wr=(ry,))
                i += ni
            if pr + 1 < NH // 2 and not qgen_done:
                qgen(pr + 1)
            if pr + 1 < NH // 2:
                for _pre in range(min(2, imax + 1)):
                    emit_qk(pr + 1, _pre)
                qk_pre[pr + 1] = True
            for hh in range(2):
                ybank, ry = (PC, rPC) if hh == 0 else (PD, rPD)
                for tt in range(NTT):
                    p.do("vector", "reciprocal", RRC[:, tt:tt + 1], ybank[:, tt * 66 + 64: tt * 66 + 65],
                         rd=(ry,), wr=(f"RRC{tt}",))
                for tt in range(NTT):
                    p.do("vector", "tensor_copy", YN[:, tt, :], ybank[:, tt * 66:(tt + 1) * 66],
                         rd=(ry,), wr=(f"YN{tt}",))
                    p.do("vector", "tensor_scalar_mul", DDIA[:, tt, :], IDBF[:], RRC[:, tt:tt + 1],
                         rd=("IDBF", f"RRC{tt}"), wr=(f"DDIA{tt}",))
                    p.do("tensor", "matmul", PQ[64 * hh:64 * hh + 64, ts(tt, 128)],
                         YN[:, tt, 0:64], DDIA[:, tt, :], start=True, stop=True,
                         rd=(f"YN{tt}", f"DDIA{tt}"), wr=(rPBF,))
            p.do("vector", "tensor_copy", YT[:, pr, :], PQ[:, 0:512], rd=(rPBF,), wr=("YT",))
        for cp in range(NCH // 2):
            co0, co1 = 2 * cp, 2 * cp + 1
            for ci in range(NCH):
                p.do("tensor", "matmul", PC[:, 0:512],
                     WCPf[:, ci * C + co0 * 128: ci * C + (co0 + 1) * 128],
                     YTf[:, ci * SEG:(ci + 1) * SEG],
                     start=(ci == 0), stop=(ci == NCH - 1), rd=("WCP", "YT"), wr=(rPC,))
                p.do("tensor", "matmul", PD[:, 0:512],
                     WCPf[:, ci * C + co1 * 128: ci * C + (co1 + 1) * 128],
                     YTf[:, ci * SEG:(ci + 1) * SEG],
                     start=(ci == 0), stop=(ci == NCH - 1), rd=("WCP", "YT"), wr=(rPD,))
            p.do("vector", "tensor_add", XT[:, co0, :], XT[:, co0, :], PC[:, 0:512],
                 rd=(rPC, "XT"), wr=("XT",))
            p.do("vector", "tensor_add", XT[:, co1, :], XT[:, co1, :], PD[:, 0:512],
                 rd=(rPD, "XT"), wr=("XT",))

    def sub_ffn(g):
        cast_xb()
        ln_stats(XB, "XB")
        claim("T1F0", "T1F1")
        for cp in range(NCH // 2):
            co0, co1 = 2 * cp, 2 * cp + 1
            for ci in range(NCH):
                p.do("tensor", "matmul", PC[:, 0:512],
                     WFTf[:, ci * C + co0 * 128: ci * C + (co0 + 1) * 128],
                     XBf[:, ci * SEG:(ci + 1) * SEG],
                     start=(ci == 0), stop=False, rd=("WFT", "XB"), wr=(rPC,))
                p.do("tensor", "matmul", PD[:, 0:512],
                     WFTf[:, ci * C + co1 * 128: ci * C + (co1 + 1) * 128],
                     XBf[:, ci * SEG:(ci + 1) * SEG],
                     start=(ci == 0), stop=False, rd=("WFT", "XB"), wr=(rPD,))
            p.do("tensor", "matmul", PC[:, 0:512], rb(cs_wft, co0 * 128, 128), rb(negmrs_row),
                 start=False, stop=True, rd=("ROWB", "negmrs_row"), wr=(rPC,))
            p.do("tensor", "matmul", PD[:, 0:512], rb(cs_wft, co1 * 128, 128), rb(negmrs_row),
                 start=False, stop=True, rd=("ROWB", "negmrs_row"), wr=(rPD,))
            for co, bank, rb_ in ((co0, PC, rPC), (co1, PD, rPD)):
                tb = co % 2
                p.do("vector", "tensor_tensor", T1F[:, tb, :], bank[:, 0:512], RSB[:], ALU.mult,
                     rd=(rb_, "RSB"), wr=(f"T1F{tb}",))
                p.do("vector", "scalar_tensor_tensor", XT[:, co, :], T1F[:, tb, :], FBC[:, co:co + 1],
                     XT[:, co, :], ALU.add, ALU.add,
                     rd=(f"T1F{tb}", "FBC", "XT"), wr=("XT",))

    def sub3(g):
        if not CFG.get("s3proj", True):
            return
        cast_xb()
        ln_stats(XB, "XB", extra=True)
        rs_transposed()
        claim("QTB", "KTB", "ZT0", "ZT1", "ET", "RT")
        for wt, rw, cs, bcol, rbc, dst, rdst in [
            (WQT, "WQT", cs_wq, BQC, "BQC", QTB, "QTB"),
            (WKT, "WKT", cs_wk, BKC, "BKC", KTB, "KTB"),
        ]:
            wtf = wt[:].rearrange("p a f -> p (a f)")
            for cp in range(NCH // 2):
                oc0, oc1 = 2 * cp, 2 * cp + 1
                for ci in range(NCH):
                    p.do("tensor", "matmul", PC[:, 0:512],
                         wtf[:, ci * C + oc0 * 128: ci * C + (oc0 + 1) * 128],
                         XBf[:, ci * SEG:(ci + 1) * SEG],
                         start=(ci == 0), stop=False, rd=(rw, "XB"), wr=(rPC,))
                    p.do("tensor", "matmul", PD[:, 0:512],
                         wtf[:, ci * C + oc1 * 128: ci * C + (oc1 + 1) * 128],
                         XBf[:, ci * SEG:(ci + 1) * SEG],
                         start=(ci == 0), stop=False, rd=(rw, "XB"), wr=(rPD,))
                p.do("tensor", "matmul", PC[:, 0:512], rb(cs, oc0 * 128, 128), rb(negmrs_row),
                     start=False, stop=True, rd=("ROWB", "negmrs_row"), wr=(rPC,))
                p.do("tensor", "matmul", PD[:, 0:512], rb(cs, oc1 * 128, 128), rb(negmrs_row),
                     start=False, stop=True, rd=("ROWB", "negmrs_row"), wr=(rPD,))
                for oc, bank, rb_ in ((oc0, PC, rPC), (oc1, PD, rPD)):
                    zb = oc % 2
                    p.do("vector", "tensor_tensor", ZT[:, zb, :], bank[:, 0:512], RSB[:], ALU.mult,
                         rd=(rb_, "RSB"), wr=(f"ZT{zb}",))
                    p.do("vector", "tensor_scalar", ET[:, 0, :], ZT[:, zb, :], bcol[:, oc:oc + 1], 0.0,
                         ALU.add, ALU.min, rd=(f"ZT{zb}", rbc), wr=("ET",))
                    p.do("scalar", "activation", ET[:, 0, :], ET[:, 0, :], AF.Exp, rd=("ET",), wr=("ET",))
                    p.do("vector", "tensor_scalar", RT[:, 0, :], ZT[:, zb, :], bcol[:, oc:oc + 1], 0.0,
                         ALU.add, ALU.max, rd=(f"ZT{zb}", rbc), wr=("RT",))
                    p.do("vector", "tensor_add", dst[:, oc, :], ET[:, 0, :], RT[:, 0, :],
                         rd=("ET", "RT"), wr=(rdst,))
        if not CFG.get("s3scan", True):
            return
        claim("BM0", "BM1", "SSB", "KNB0", "KNB1", "VNB0", "VNB1")
        for c in range(NTT):
            Cg = 4 * g + c
            kb = c % 2
            for ci in range(NCH):
                p.do("tensor", "matmul", PC[:, 0:512],
                     XBf[:, ci * SEG + c * 128: ci * SEG + (c + 1) * 128],
                     WVTf[:, ci * C: ci * C + 512],
                     start=(ci == 0), stop=False, rd=("WVT", "XB"), wr=(rPC,))
                p.do("tensor", "matmul", PD[:, 0:512],
                     XBf[:, ci * SEG + c * 128: ci * SEG + (c + 1) * 128],
                     WVTf[:, ci * C + 512: ci * C + 1024],
                     start=(ci == 0), stop=False, rd=("WVT", "XB"), wr=(rPD,))
            for half, bank, rb_ in ((0, PC, rPC), (1, PD, rPD)):
                p.do("tensor", "matmul", bank[:, 0:512], rb(negmrs_row, c * 128, 128), rb(cs_wv, half * 512, 512),
                     start=False, stop=False, rd=("ROWB", "negmrs_row"), wr=(rb_,))
                p.do("tensor", "matmul", bank[:, 0:512], rb(invrs_row, c * 128, 128), rb(bv_row, half * 512, 512),
                     start=False, stop=True, rd=("ROWB", "invrs_row"), wr=(rb_,))
                p.do("vector", "tensor_scalar_mul", VNB[:, kb, ts(half, 512)], bank[:, 0:512], RST[:, c:c + 1],
                     rd=(rb_, "RST"), wr=(f"VNB{kb}",))
            for h2 in range(NCH if CFG.get("s3knat", True) else 0):
                p.do("tensor", "transpose", PBF[:, ts(h2, 128)],
                     KTBf[:, h2 * SEG + c * 128: h2 * SEG + (c + 1) * 128], IDBF[:],
                     rd=("KTB", "IDBF"), wr=(rPBF,))
            if CFG.get("s3knat", True):
                p.do("vector", "tensor_copy", KNB[:, kb, :], PBF[:], rd=(rPBF,), wr=(f"KNB{kb}",))
            else:
                p.do("vector", "memset", KNB[:, kb, :], 0.0, wr=(f"KNB{kb}",))
            if Cg > 0:
                p.do("vector", "tensor_copy", SSB[:], SPS[:], rd=(rSPS,), wr=("SSB",))
            else:
                p.do("vector", "memset", SSB[:], 0.0, wr=("SSB",))
            def emit_b(bq4):
                for hh in range(4):
                    h = bq4 * 4 + hh
                    pbase = 64 * (h % 2)
                    bbank, rbb = (PC, rPC) if h % 2 == 0 else (PD, rPD)
                    p.do("tensor", "matmul", bbank[:, (hh // 2) * 128:(hh // 2) * 128 + 128],
                         KTBf[pbase:pbase + 64, (h // 2) * SEG + c * 128: (h // 2) * SEG + (c + 1) * 128],
                         QTBf[pbase:pbase + 64, (h // 2) * SEG + c * 128: (h // 2) * SEG + (c + 1) * 128],
                         start=True, stop=True, tile_position=(pbase, 0), rd=("KTB", "QTB"), wr=(rbb,))
            if CFG.get("s3b", True):
                emit_b(0)
            for bq4 in range(4 if CFG.get("s3b", True) else 0):
                bmb = bq4 % 2
                msk2 = TRIU[:].rearrange("p (a f) -> p a f", a=1).broadcast_to([128, 2, 128])
                BMv = BM[:, bmb].rearrange("p h f -> p (h f)").rearrange("p (a b f) -> p a b f", a=2, b=2)
                p.do("vector", "tensor_tensor", BMv[:, :, 0, :],
                     PC[:, 0:256].rearrange("p (a f) -> p a f", a=2), msk2, ALU.mult,
                     rd=(rPC, "TRIU"), wr=(f"BM{bmb}",))
                p.do("vector", "tensor_tensor", BMv[:, :, 1, :],
                     PD[:, 0:256].rearrange("p (a f) -> p a f", a=2), msk2, ALU.mult,
                     rd=(rPD, "TRIU"), wr=(f"BM{bmb}",))
                if bq4 + 1 < 4:
                    emit_b(bq4 + 1)
                for hh in range(4 if CFG.get("s3intra", True) else 0):
                    h = bq4 * 4 + hh
                    pbase = 64 * (h % 2)
                    oslot = (h // 2) * 128
                    rpb = (rPB0,) if oslot < 512 else (rPB1,)
                    p.do("tensor", "matmul", PB[pbase:pbase + 64, oslot:oslot + 128],
                         VNBf[:, kb * C + h * 64: kb * C + (h + 1) * 64],
                         BM[:].rearrange("p a b f -> p (a b f)")[:, (bq4 % 2) * 512 + hh * 128: (bq4 % 2) * 512 + (hh + 1) * 128],
                         start=True, stop=False, rd=(f"BM{bq4 % 2}", f"VNB{kb}"), wr=rpb)
                    if CFG.get("s3inter", True):
                        p.do("tensor", "matmul", PB[pbase:pbase + 64, oslot:oslot + 128],
                             SSB[pbase:pbase + 64, (h // 2) * 64:(h // 2) * 64 + 64],
                             QTBf[pbase:pbase + 64, (h // 2) * SEG + c * 128: (h // 2) * SEG + (c + 1) * 128],
                             start=False, stop=True, tile_position=(pbase, pbase), rd=("SSB", "QTB"), wr=rpb)
                    else:
                        p.do("tensor", "matmul", PB[pbase:pbase + 64, oslot:oslot + 128],
                             VNB[:, kb, h * 64:(h + 1) * 64], BM[:, bq4 % 2, hh, :],
                             start=False, stop=True, rd=(f"BM{bq4 % 2}", f"VNB{kb}"), wr=rpb)
            for h in range(NH if CFG.get("s3supd", True) else 0):
                pbase = 64 * (h % 2)
                p.do("tensor", "matmul", SPS[pbase:pbase + 64, (h // 2) * 64:(h // 2) * 64 + 64],
                     KNBf[:, kb * C + h * 64: kb * C + (h + 1) * 64],
                     VNBf[:, kb * C + h * 64: kb * C + (h + 1) * 64],
                     start=(Cg == 0 and h == 0), stop=(Cg == T // 128 - 1 and h == NH - 1),
                     skip_group_check=True,
                     rd=(f"KNB{kb}", f"VNB{kb}"), wr=(rSPS,))
            if CFG.get("s3oevac", True):
                p.do("vector", "tensor_copy", XB[:, :, ts(c, 128)],
                     PB[:].rearrange("p (a f) -> p a f", a=8), rd=(rPB0, rPB1), wr=("OB",))
        if not CFG.get("s3out", True):
            return
        ln_stats(XB, "OB", extra=True)
        claim("T1F0", "T1F1")
        p.do("tensor", "matmul", PB[:, 0:512], rf(onesf_row), rf(negmf_row), start=True, stop=True,
             rd=("negmf_row", "ROWF"), wr=(rPB0,))
        p.do("vector", "tensor_copy", T1F[:, 0, :], PB[:, 0:512], rd=(rPB0,), wr=("T1F0",))
        for ci in range(NCH):
            p.do("vector", "tensor_add", XB[:, ci, :], XB[:, ci, :], T1F[:, 0, :],
                 rd=("OB", "T1F0"), wr=("OB",))
        for cp in range(NCH // 2):
            co0, co1 = 2 * cp, 2 * cp + 1
            for ci in range(NCH):
                p.do("tensor", "matmul", PC[:, 0:512],
                     WGTf[:, ci * C + co0 * 128: ci * C + (co0 + 1) * 128],
                     XBf[:, ci * SEG:(ci + 1) * SEG],
                     start=(ci == 0), stop=(ci == NCH - 1), rd=("WGT", "OB"), wr=(rPC,))
                p.do("tensor", "matmul", PD[:, 0:512],
                     WGTf[:, ci * C + co1 * 128: ci * C + (co1 + 1) * 128],
                     XBf[:, ci * SEG:(ci + 1) * SEG],
                     start=(ci == 0), stop=(ci == NCH - 1), rd=("WGT", "OB"), wr=(rPD,))
            for co, bank, rb_ in ((co0, PC, rPC), (co1, PD, rPD)):
                tb = co % 2
                p.do("vector", "tensor_tensor", T1F[:, tb, :], bank[:, 0:512], RSB[:], ALU.mult,
                     rd=(rb_, "RSB"), wr=(f"T1F{tb}",))
                p.do("vector", "scalar_tensor_tensor", XT[:, co, :], T1F[:, tb, :], BWOC[:, co:co + 1],
                     XT[:, co, :], ALU.add, ALU.add,
                     rd=(f"T1F{tb}", "BWOC", "XT"), wr=("XT",))

    def trans_out(g):
        claim("XONT0", "XONT1")
        for tt in range(NTT):
            xb_ = tt % 2
            for half in range(2):
                bank, rb_ = (PC, rPC) if half == 0 else (PD, rPD)
                for ci in range(4 * half, 4 * half + 4):
                    p.do("tensor", "transpose", bank[:, ts(ci % 4, 128)], XT[:, ci, ts(tt, 128)], IDF[:],
                         rd=("XT", "IDF"), wr=(rb_,))
                p.do("vector", "tensor_copy", XON[:, xb_, ts(half, 512)], bank[:, 0:512],
                     rd=(rb_,), wr=(f"XONT{xb_}",))
            rows = d_out[g * SEG + tt * 128: g * SEG + (tt + 1) * 128, :]
            p.do("sync", "dma_start", out=rows, in_=XON[:, xb_, :], rd=(f"XONT{xb_}",), wr=("outdma",))

    startup()
    for g in range(NSEG):
        trans_in(g)
        if CFG["sub1"]:
            sub1(g)
        if CFG["sub2"]:
            sub_ffn(g)
        if CFG["sub3"]:
            sub3(g)
        if g + 1 < NSEG:
            load_tile(g + 1, 0)
            load_tile(g + 1, 1)
        if CFG["sub4"]:
            sub_ffn(g)
        trans_out(g)
    p.do("sync", "dma_start", out=d_dbgb, in_=ROWB[:], rd=("ROWB", "negmrs_row"), wr=("dbg",))
    p.do("sync", "dma_start", out=d_dbgw, in_=WFT[:, 0, :], rd=("WFT",), wr=("dbg",))
    p.do("sync", "dma_start", out=d_dbgo, in_=XB[:], rd=("OB", "XB"), wr=("dbg",))
    p.do("sync", "dma_start", out=d_dbgs, in_=SSB[:], rd=("SSB",), wr=("dbg",))
    p.do("sync", "dma_start", out=d_dbga, in_=DBGS[:, 0:9], rd=("DBGS",), wr=("dbg",))
    for i, (t, r) in enumerate([(DELC[:], "DELC"), (NDELC[:], "NDELC"), (ALFC[:], "ALFC"), (BQC[:, 0:1], "BQC"),
                                (FBC[:, 0:1], "FBC"), (BWOC[:, 0:1], "BWOC"), (RSB[:, 0:1], "RSB"),
                                (RST[:, 0:1], "RST")]):
        p.do("sync", "dma_start", out=d_dbgf[:, i:i + 1], in_=t, rd=(r,), wr=("dbg",),
             allow_slow_non_contiguous=True)
    final = p.ops["sync"][-1]
    assert final.fn == "dma_start"
    p.barrier_wait("sync", final)

    p.finalize_and_emit()
    return nc


_NC_CACHE = None


def make_consts():
    s = np.arange(128)[:, None]
    u = np.arange(896)[None, :]
    mbig = (s <= (u - 384)).astype(ml_dtypes.bfloat16)
    tj = np.arange(128)[:, None]
    ti = np.arange(128)[None, :]
    triu = (tj <= ti).astype(ml_dtypes.bfloat16)
    idm = np.eye(128).astype(ml_dtypes.bfloat16)
    cbf = np.concatenate([mbig, triu, idm], axis=1)
    cf32 = np.eye(128, dtype=np.float32)
    return cbf, cf32


def kernel(L0_attn_a_x, L0_attn_a_Wc_down, L0_attn_a_Wq_up, L0_attn_a_Wkv_up, L0_attn_a_Wc_proj,
           L0_attn_b_Wq, L0_attn_b_bq, L0_attn_b_Wk, L0_attn_b_bk, L0_attn_b_Wv, L0_attn_b_bv,
           L0_attn_b_Wo, L0_attn_b_bo, L0_attn_b_ln_g, L0_attn_b_ln_b,
           L0_ffn_weight, L0_ffn_alpha, L0_ffn_bias):
    global _NC_CACHE
    x = np.ascontiguousarray(np.asarray(L0_attn_a_x, dtype=np.float32))
    B = x.shape[0]
    cbf, cf32 = make_consts()
    weights = dict(
        wcd=np.ascontiguousarray(L0_attn_a_Wc_down, np.float32),
        wqup=np.ascontiguousarray(L0_attn_a_Wq_up, np.float32),
        wkv=np.ascontiguousarray(L0_attn_a_Wkv_up, np.float32),
        wcp=np.ascontiguousarray(L0_attn_a_Wc_proj, np.float32),
        wq=np.ascontiguousarray(L0_attn_b_Wq, np.float32),
        wk=np.ascontiguousarray(L0_attn_b_Wk, np.float32),
        wv=np.ascontiguousarray(L0_attn_b_Wv, np.float32),
        wo=np.ascontiguousarray(L0_attn_b_Wo, np.float32),
        bq=np.ascontiguousarray(L0_attn_b_bq, np.float32),
        bk=np.ascontiguousarray(L0_attn_b_bk, np.float32),
        bv=np.ascontiguousarray(L0_attn_b_bv, np.float32),
        bo=np.ascontiguousarray(L0_attn_b_bo, np.float32),
        lng=np.ascontiguousarray(L0_attn_b_ln_g, np.float32),
        lnb=np.ascontiguousarray(L0_attn_b_ln_b, np.float32),
        fw=np.ascontiguousarray(L0_ffn_weight, np.float32),
        falpha=np.ascontiguousarray(L0_ffn_alpha, np.float32),
        fbias=np.ascontiguousarray(L0_ffn_bias, np.float32),
        cbf=cbf, cf32=cf32,
    )
    if _NC_CACHE is None:
        _NC_CACHE = build_program()
    nc = _NC_CACHE
    in_maps = []
    for b in range(B):
        m = dict(weights)
        m["x"] = np.ascontiguousarray(x[b])
        in_maps.append(m)
    res = run_bass_kernel_spmd(nc, in_maps, core_ids=list(range(B)))
    out = np.stack([r["out"] for r in res.results], axis=0)
    return out


# version 12 (final)
# speedup vs baseline: 1.0271x; 1.0271x over previous
import numpy as np
import ml_dtypes
import concourse.bass as bass
import concourse.mybir as mybir
from concourse.bass_utils import run_bass_kernel_spmd

F32 = mybir.dt.float32
BF16 = mybir.dt.bfloat16
AF = mybir.ActivationFunctionType
ALU = mybir.AluOpType
AX = mybir.AxisListType

T = 4096
C = 1024
NH = 16
HD = 64
LD = 512
SEG = 512
NSEG = T // SEG
NTT = SEG // 128
NCH = C // 128
CSH = 4.0
EPS = 1e-5

ENGINES = ("tensor", "scalar", "vector", "gpsimd", "sync")

CFG = dict(sub1=True, sub2=True, sub3=True, sub4=True, s3proj=True, s3scan=True, s3out=True)


class Op:
    __slots__ = ("eng", "fn", "args", "kwargs", "waits", "need_inc", "val", "sem", "amt", "raw", "idx")

    def __init__(self, eng, fn, args, kwargs, sem, amt, raw=False):
        self.eng, self.fn, self.args, self.kwargs = eng, fn, args, kwargs
        self.waits = set()
        self.need_inc = False
        self.val = None
        self.sem = sem
        self.amt = amt
        self.raw = raw
        self.idx = -1


class Prog:
    def __init__(self, nc):
        self.nc = nc
        self.ops = {e: [] for e in ENGINES}
        self.lastw = {}
        self.readers = {}
        self.semh = {}
        for name in ("spe", "sact", "sdve", "sgp", "dsync", "dgp"):
            self.semh[name] = nc.semaphore(name).__enter__()

    def _sem_for(self, eng, fn):
        if fn in ("dma_start", "dma_start_transpose"):
            return ("dgp", 16) if eng == "gpsimd" else ("dsync", 16)
        return ({"tensor": "spe", "scalar": "sact", "vector": "sdve",
                 "gpsimd": "sgp", "sync": "dsync"}[eng], 1)

    def do(self, eng, fn, *args, rd=(), wr=(), **kwargs):
        sem, amt = self._sem_for(eng, fn)
        op = Op(eng, fn, args, kwargs, sem, amt)
        if amt == 16:
            op.need_inc = True
        deps = set()
        for r in rd:
            d = self.lastw.get(r)
            if d is not None:
                deps.add(d)
        for w in wr:
            d = self.lastw.get(w)
            if d is not None:
                deps.add(d)
            for d2 in self.readers.get(w, ()):
                deps.add(d2)
        my_idx = len(self.ops[eng])
        op.idx = my_idx
        for d in deps:
            if d is op:
                continue
            if d.eng != eng:
                d.need_inc = True
                op.waits.add(d)
            else:
                dist = my_idx - d.idx
                slow = d.fn in ("reduce_sum", "tensor_reduce", "reciprocal")
                if dist <= (2 if slow else 1) and not d.raw:
                    d.need_inc = True
                    op.waits.add(d)
        self.ops[eng].append(op)
        for w in wr:
            self.lastw[w] = op
            self.readers[w] = []
        for r in rd:
            self.readers.setdefault(r, []).append(op)
        return op

    def barrier_wait(self, eng, target_op):
        target_op.need_inc = True
        op = Op(eng, None, None, None, None, None, raw=True)
        op.waits.add(target_op)
        self.ops[eng].append(op)
        return op

    def finalize_and_emit(self):
        for eng in ENGINES:
            for op in self.ops[eng]:
                if not op.waits:
                    continue
                best = {}
                for d in op.waits:
                    k = (d.eng, d.sem)
                    if k not in best or d.idx > best[k].idx:
                        best[k] = d
                op.waits = set(best.values())
        for eng in ENGINES:
            for op in self.ops[eng]:
                op.need_inc = (op.amt == 16) if not op.raw else False
        for eng in ENGINES:
            for op in self.ops[eng]:
                for d in op.waits:
                    d.need_inc = True
        counts = {}
        for eng in ENGINES:
            for op in self.ops[eng]:
                if op.raw:
                    continue
                if op.need_inc:
                    counts[op.sem] = counts.get(op.sem, 0) + op.amt
                    op.val = counts[op.sem]
        nc = self.nc
        semh = self.semh

        def emitter(eng):
            def run(e):
                water = {}
                for op in self.ops[eng]:
                    for d in sorted(op.waits, key=lambda o: (o.sem, o.val)):
                        assert d.val is not None, f"dep without inc: {d.eng}.{d.fn}"
                        if water.get(d.sem, 0) < d.val:
                            e.wait_ge(semh[d.sem], d.val)
                            water[d.sem] = d.val
                    if op.raw:
                        continue
                    ins = getattr(e, op.fn)(*op.args, **op.kwargs)
                    if op.need_inc:
                        ins.then_inc(semh[op.sem], op.amt)
            return run

        with nc.Block() as block:
            for eng in ENGINES:
                getattr(block, eng)(emitter(eng))


def ts(i, n):
    return slice(i * n, (i + 1) * n)


def build_program():
    nc = bass.Bass("TRN2", target_bir_lowering=False)

    d_x = nc.dram_tensor("x", [T, C], F32, kind="ExternalInput").ap()
    d_out = nc.dram_tensor("out", [T, C], F32, kind="ExternalOutput").ap()
    dw = {}
    for name, shape in [
        ("wcd", [C, LD]), ("wqup", [LD, C]), ("wkv", [LD, 2 * HD]), ("wcp", [C, C]),
        ("wq", [C, C]), ("wk", [C, C]), ("wv", [C, C]), ("wo", [C, C]),
        ("bq", [C]), ("bk", [C]), ("bv", [C]), ("bo", [C]),
        ("lng", [C]), ("lnb", [C]),
        ("fw", [C, C]), ("falpha", [1]), ("fbias", [C]),
    ]:
        dw[name] = nc.dram_tensor(name, shape, F32, kind="ExternalInput").ap()
    d_dbgb = nc.dram_tensor("dbgb", [1, 8192], BF16, kind="ExternalOutput").ap()
    d_dbgw = nc.dram_tensor("dbgw", [128, 1024], BF16, kind="ExternalOutput").ap()
    d_dbgo = nc.dram_tensor("dbgo", [128, 8, 512], BF16, kind="ExternalOutput").ap()
    d_dbgs = nc.dram_tensor("dbgs", [128, 512], BF16, kind="ExternalOutput").ap()
    d_dbga = nc.dram_tensor("dbga", [128, 9], F32, kind="ExternalOutput").ap()
    d_dbgf = nc.dram_tensor("dbgf", [128, 16], F32, kind="ExternalOutput").ap()
    d_cbf = nc.dram_tensor("cbf", [128, 896 + 128 + 128], BF16, kind="ExternalInput").ap()
    d_cf32 = nc.dram_tensor("cf32", [128, 128], F32, kind="ExternalInput").ap()

    A = nc.alloc_sbuf_tensor
    XT = A("XT", [128, NCH, SEG], F32)
    XB = A("XB", [128, NCH, SEG], BF16)
    KDUP = A("KDUP", [128, T], BF16)
    VEXT = A("VEXT", [128, T // 128, 66], BF16)
    WCD = A("WCD", [128, NCH, LD], BF16)
    WQUP = A("WQUP", [128, LD // 128, C], BF16)
    WKV = A("WKV", [128, LD // 128, 2 * HD], BF16)
    WCP = A("WCP", [128, NCH, C], BF16)
    WFT = A("WFT", [128, NCH, C], BF16)
    WQT = A("WQT", [128, NCH, C], BF16)
    WKT = A("WKT", [128, NCH, C], BF16)
    WVT = A("WVT", [128, NCH, C], BF16)
    WGT = A("WGT", [128, NCH, C], BF16)
    MBIG = A("MBIG", [128, 896], BF16)
    TRIU = A("TRIU", [128, 128], BF16)
    IDBF = A("IDBF", [128, 128], BF16)
    IDF = A("IDF", [128, 128], F32)
    ONESC = A("ONESC", [128, 1], BF16)
    ONESCF = A("ONESCF", [128, 1], F32)
    CEXP = A("CEXP", [128, 2], F32)
    CNH = A("CNH", [128, 1], F32)
    CEPS = A("CEPS", [128, 1], F32)
    BQC = A("BQC", [128, NCH], F32)
    BKC = A("BKC", [128, NCH], F32)
    LNGC = A("LNGC", [128, NCH], F32)
    LNBCB = A("LNBCB", [128, NCH], BF16)
    FBC = A("FBC", [128, NCH], F32)
    BWOC = A("BWOC", [128, NCH], F32)
    DBGS = A("DBGS", [128, 10], F32)
    DELC = A("DELC", [128, 1], F32)
    NDELC = A("NDELC", [128, 1], F32)
    ALFC = A("ALFC", [128, 1], F32)
    ROWB = A("ROWB", [1, 8192], BF16)
    ROWF = A("ROWF", [1, 1280], F32)
    ROWD = A("ROWD", [1, 1024], F32)
    RSB = A("RSB", [128, SEG], F32)
    RST = A("RST", [128, NTT], F32)
    QTB = A("QTB", [128, NCH, SEG], BF16)
    KTB = A("KTB", [128, NCH, SEG], BF16)
    KNB = A("KNB", [128, 2, C], BF16)
    VNB = A("VNB", [128, 2, C], BF16)
    ZT = A("ZT", [128, 2, SEG], BF16)
    ET = A("ET", [128, 1, SEG], BF16)
    RT = A("RT", [128, 1, SEG], BF16)
    SQ = QTB

    def _addr(h):
        return nc.lookup_mloc(h).addr

    aq, ak = _addr(QTB), _addr(KTB)
    akn, avn = _addr(KNB), _addr(VNB)
    az, ae, ar = _addr(ZT), _addr(ET), _addr(RT)
    assert ak == aq + 8192 and avn == akn + 4096 and ae == az + 2048 and ar == ae + 1024
    AT = nc.alloc_sbuf_tensor_at
    LATB = AT("LATB", [128, LD // 128, SEG], BF16, offset=aq)
    QH = AT("QH", [128, 2, SEG], BF16, offset=aq + 4096)
    EXPB = AT("EXPB", [128, 4, 1024], BF16, offset=ak)
    XIN = AT("XIN", [128, 2, 1024], F32, offset=ak)
    WSTG = AT("WSTG", [128, 4, C], F32, offset=aq)
    WSTGB = AT("WSTGB", [128, 4, C], BF16, offset=akn)
    YT = AT("YT", [128, NCH, SEG], BF16, offset=akn)
    XON = AT("XON", [128, 2, C], F32, offset=akn)
    YN = AT("YN", [128, 4, 66], BF16, offset=az)
    DDIA = AT("DDIA", [128, 4, 128], BF16, offset=az + 544)
    RRC = AT("RRC", [128, 4], F32, offset=az + 1568)
    BM = AT("BM", [128, 2, 4, 128], BF16, offset=az)
    SSB = AT("SSB", [128, 512], BF16, offset=az + 2048)
    T1F = AT("T1F", [128, 2, SEG], F32, offset=az)
    ABSA = AT("ABSA", [128, NCH + 1], F32, offset=_addr(RSB))
    WSTGF0 = AT("WSTGF0", [128, 4, C], F32, offset=_addr(XT))
    WSTGF1 = AT("WSTGF1", [128, 4, C], F32, offset=_addr(WQT))
    WSTGB_A = AT("WSTGB_A", [128, 4, C], BF16, offset=aq)
    WSTGB_XB = AT("WSTGB_XB", [128, 4, C], BF16, offset=_addr(XB))
    WSTGB_KD = AT("WSTGB_KD", [128, 4, C], BF16, offset=_addr(KDUP))
    WSTGB_G1 = AT("WSTGB_G1", [128, 4, C], BF16, offset=_addr(WGT))
    WSTGB_G2 = AT("WSTGB_G2", [128, 4, C], BF16, offset=_addr(WGT) + 8192)

    rem = nc.sbuf_bytes_remaining
    assert rem >= 0, f"SBUF over budget by {-rem} bytes"

    PTAB = nc.alloc_psum_tensor("PTAB", [128, 2048], F32)
    PA = PTAB[:, 0:1024]
    PB = PTAB[:, 1024:2048]
    PC = nc.alloc_psum_tensor("PC", [128, 512], F32)
    PD = nc.alloc_psum_tensor("PD", [128, 512], F32)
    PBF = nc.alloc_psum_tensor("PBF", [128, 1024], BF16)
    SPS = nc.alloc_psum_tensor("SPS", [128, 512], F32)

    p = Prog(nc)
    rPA0, rPA1, rPB0, rPB1, rPC, rPD, rPBF, rSPS = "PT0", "PT1", "PT2", "PT3", "PC", "PD", "PBF", "SPS"

    def _flat(t):
        return t[:].rearrange("p a f -> p (a f)")

    XBf = _flat(XB)
    WCDf = _flat(WCD)
    WQUPf = _flat(WQUP)
    WKVf = _flat(WKV)
    WCPf = _flat(WCP)
    WFTf = _flat(WFT)
    WVTf = _flat(WVT)
    WGTf = _flat(WGT)
    LATBf = _flat(LATB)
    QTBf = _flat(QTB)
    KTBf = _flat(KTB)
    KNBf = _flat(KNB)
    VNBf = _flat(VNB)
    YTf = _flat(YT)

    R1 = ["QTB", "KTB", "LATB", "QH0", "QH1", "EXPB0", "EXPB1", "EXPB2", "EXPB3",
          "XIN0", "XIN1", "WSTG", "WSTGB_A", "XONQ"]
    R2 = ["KNB0", "KNB1", "VNB0", "VNB1", "YT", "XONT0", "XONT1", "WSTGB"]
    R6 = ["XB", "OB", "WSTGB_XB"]
    R7 = ["KDUP", "WSTGB_KD"]
    R8 = ["WGT", "WSTGB_G1", "WSTGB_G2"]
    R9 = ["WQT", "WSTGF1"]
    R3 = ["ZT0", "ZT1", "ET", "RT", "BM0", "BM1", "SSB", "T1F0", "T1F1",
          "YN0", "YN1", "YN2", "YN3", "DDIA0", "DDIA1", "DDIA2", "DDIA3",
          "RRC0", "RRC1", "RRC2", "RRC3"]
    R4 = ["RSB", "ABSA"]
    REGION = {}
    for lst in (R1, R2, R3, R4, R6, R7, R8, R9):
        for n in lst:
            REGION[n] = lst

    def claim(*names):
        for new in names:
            lst = p.readers.setdefault(new, [])
            for o in REGION.get(new, ()):
                if o == new:
                    continue
                if o in p.lastw:
                    lst.append(p.lastw[o])
                lst.extend(p.readers.get(o, ()))

    _off = {"b": 0, "f": 0, "d": 0}

    def _row(kind, width, cap):
        off = _off[kind]
        _off[kind] = off + width
        assert _off[kind] <= cap, kind
        return (off, width)

    def rb(slot, lo=0, n=None):
        off, w = slot
        n = w if n is None else n
        return ROWB[0:1, off + lo: off + lo + n]

    def rf(slot, lo=0, n=None):
        off, w = slot
        n = w if n is None else n
        return ROWF[0:1, off + lo: off + lo + n]

    def rd_(slot, lo=0, n=None):
        off, w = slot
        n = w if n is None else n
        return ROWD[0:1, off + lo: off + lo + n]

    ones_row = _row("b", 512, 8192)
    cs_wcd = _row("b", 512, 8192)
    cs_wft = _row("b", 1024, 8192)
    cs_wq = _row("b", 1024, 8192)
    cs_wk = _row("b", 1024, 8192)
    cs_wv = _row("b", 1024, 8192)
    cs_wg = _row("b", 1024, 8192)
    bv_row = _row("b", 1024, 8192)
    negmrs_row = _row("b", 512, 8192)
    invrs_row = _row("b", 512, 8192)
    onesf_row = _row("f", 128, 1280)
    rs_row = _row("f", 512, 1280)
    negmf_row = _row("f", 512, 1280)
    drow_s = _row("f", 1, 1280)
    arow_s = _row("f", 1, 1280)
    m_row = _row("d", 512, 1024)
    s_row = _row("d", 512, 1024)

    def startup():
        p.do("sync", "dma_start", out=MBIG[:], in_=d_cbf[:, 0:896], wr=("MBIG",))
        p.do("sync", "dma_start", out=TRIU[:], in_=d_cbf[:, 896:1024], wr=("TRIU",))
        p.do("sync", "dma_start", out=IDBF[:], in_=d_cbf[:, 1024:1152], wr=("IDBF",))
        p.do("sync", "dma_start", out=IDF[:], in_=d_cf32[:], wr=("IDF",))
        p.do("vector", "memset", ONESC[:], 1.0, wr=("ONESC",))
        p.do("vector", "memset", ONESCF[:], 1.0, wr=("ONESCF",))
        p.do("vector", "memset", CEXP[:, 0:1], -CSH, wr=("CEXP",))
        p.do("vector", "memset", CEXP[:, 1:2], 0.125, wr=("CEXP",))
        p.do("vector", "memset", CNH[:], -0.5, wr=("CNH",))
        p.do("vector", "memset", CEPS[:], EPS, wr=("CEPS",))
        p.do("vector", "memset", rb(ones_row), 1.0, wr=("ROWB",))
        p.do("vector", "memset", rf(onesf_row), 1.0, wr=("ROWF",))
        p.do("vector", "memset", VEXT[:, :, 64:65], 1.0, wr=("VEXT_ones",))
        p.do("vector", "memset", VEXT[:, :, 65:66], 0.0, wr=("VEXT_ones",))

        p.do("gpsimd", "dma_start", out=WCD[:],
             in_=dw["wcd"].rearrange("(a p) f -> p a f", p=128), wr=("WCD",))
        p.do("gpsimd", "dma_start", out=WQUP[:],
             in_=dw["wqup"].rearrange("(a p) f -> p a f", p=128), wr=("WQUP",))
        p.do("gpsimd", "dma_start", out=WKV[:],
             in_=dw["wkv"].rearrange("(a p) f -> p a f", p=128), wr=("WKV",))
        p.do("gpsimd", "dma_start", out=WCP[:],
             in_=dw["wcp"].rearrange("(a p) f -> p a f", p=128), wr=("WCP",))
        load_tile(0, 0)
        load_tile(0, 1)
        p.do("gpsimd", "dma_start", out=WSTGF0[:],
             in_=dw["fw"][0:512, :].rearrange("(a p) f -> p a f", p=128), wr=("WSTGF0", "XT"))
        p.do("gpsimd", "dma_start", out=WSTGF1[:],
             in_=dw["fw"][512:1024, :].rearrange("(a p) f -> p a f", p=128), wr=("WSTGF1", "WQT"))
        for stg, rstg, srcn, half in [(WSTGB_A, "WSTGB_A", "wq", 0), (WSTGB_XB, "WSTGB_XB", "wq", 1),
                                      (WSTGB_KD, "WSTGB_KD", "wk", 0), (WSTGB_G1, "WSTGB_G1", "wk", 1),
                                      (WSTGB_G2, "WSTGB_G2", "wv", 0)]:
            p.do("gpsimd", "dma_start", out=stg[:],
                 in_=dw[srcn][half * 512:(half + 1) * 512, :].rearrange("(a p) f -> p a f", p=128),
                 wr=(rstg,))
        p.do("gpsimd", "dma_start", out=rb(bv_row),
             in_=dw["bv"].rearrange("(a f) -> a f", a=1), wr=("ROWB",))
        p.do("gpsimd", "dma_start", out=rf(arow_s),
             in_=dw["falpha"].rearrange("(a f) -> a f", a=1), wr=("arow",))

        p.do("vector", "memset", ABSA[:], 0.0, wr=("ABSA",))
        for a in range(4):
            p.do("vector", "reduce_sum", ABSA[:, a:a + 1], WSTGF0[:, a, :],
                 axis=AX.X, apply_absolute_value=True, rd=("WSTGF0", "XT"), wr=("ABSA",))
        for a in range(4):
            p.do("vector", "reduce_sum", ABSA[:, 4 + a:4 + a + 1], WSTGF1[:, a, :],
                 axis=AX.X, apply_absolute_value=True, rd=("WSTGF1", "WQT"), wr=("ABSA",))
        p.do("vector", "memset", DBGS[:, 9:10], 0.0, wr=("PAD",))
        p.do("vector", "memset", DBGS[:, 9:10], 1.0, wr=("PAD",))
        p.do("vector", "reduce_sum", ABSA[:, 8:9], ABSA[:, 0:8], axis=AX.X, rd=("ABSA",), wr=("ABSA1",))
        p.do("vector", "memset", DBGS[:, 9:10], 0.0, wr=("PAD",))
        p.do("vector", "memset", DBGS[:, 9:10], 1.0, wr=("PAD",))
        p.do("vector", "tensor_copy", DBGS[:, 0:9], ABSA[:], rd=("ABSA", "ABSA1"), wr=("DBGS",))
        p.do("tensor", "matmul", PC[0:1, 0:1], ONESCF[:], ABSA[:, 8:9], start=True, stop=True,
             rd=("ABSA1", "ONESCF"), wr=(rPC,))
        p.do("vector", "tensor_scalar_mul", rf(drow_s), PC[0:1, 0:1], 0.7 / (C * C),
             rd=(rPC,), wr=("drow",))
        p.do("tensor", "matmul", PD[:, 0:1], rf(onesf_row), rf(drow_s), start=True, stop=True,
             rd=("drow", "ROWF"), wr=(rPD,))
        p.do("vector", "tensor_copy", DELC[:], PD[:, 0:1], rd=(rPD,), wr=("DELC",))
        p.do("vector", "tensor_scalar_mul", NDELC[:], DELC[:], -1.0, rd=("DELC",), wr=("NDELC",))
        p.do("tensor", "matmul", PD[:, 1:2], rf(onesf_row), rf(arow_s), start=True, stop=True,
             rd=("arow", "ROWF"), wr=(rPD,))
        p.do("vector", "tensor_copy", ALFC[:], PD[:, 1:2], rd=(rPD,), wr=("ALFC",))

        TROT = [(PBF[:], rPBF), (PC[:].bitcast(BF16), rPC), (PD[:].bitcast(BF16), rPD)]

        def wtrans(stg, rstg, dst, rdst, half):
            for ci in range(NCH):
                bank, rb_ = TROT[ci % 3]
                for cj in range(4):
                    p.do("tensor", "transpose", bank[:, ts(cj, 128)], stg[:, cj, ts(ci, 128)],
                         IDBF[:], rd=(rstg, "IDBF"), wr=(rb_,))
                p.do("vector", "tensor_copy", dst[:, ci, ts(half, 512)], bank[:, 0:512],
                     rd=(rb_,), wr=(rdst,))
        wtrans(WSTGB_KD, "WSTGB_KD", WKT, "WKT", 0)
        wtrans(WSTGB_G1, "WSTGB_G1", WKT, "WKT", 1)
        wtrans(WSTGB_G2, "WSTGB_G2", WVT, "WVT", 0)

        claim("WSTGB")
        for half in range(2):
            SRC, rsrc_, rx = (WSTGF0, "WSTGF0", "XT") if half == 0 else (WSTGF1, "WSTGF1", "WQT")
            for a in range(4):
                for hb in range(2):
                    sl = slice(hb * 512, (hb + 1) * 512)
                    p.do("vector", "tensor_scalar", WSTGB[:, a, sl], SRC[:, a, sl], DELC[:], None,
                         ALU.is_gt, rd=(rsrc_, rx, "DELC"), wr=("c1",))
                    p.do("vector", "tensor_scalar", ZT[:, 0, :], SRC[:, a, sl], NDELC[:], None,
                         ALU.is_lt, rd=(rsrc_, rx, "NDELC"), wr=("c2",))
                    p.do("vector", "tensor_sub", WSTGB[:, a, sl], WSTGB[:, a, sl], ZT[:, 0, :],
                         rd=("c1", "c2"), wr=("c1",))
                    p.do("vector", "tensor_scalar_mul", WSTGB[:, a, sl], WSTGB[:, a, sl], ALFC[:],
                         rd=("c1", "ALFC"), wr=("WSTGB",))
            for ci in range(NCH):
                bank, rb_ = TROT[ci % 3]
                for cj in range(4):
                    p.do("tensor", "transpose", bank[:, ts(cj, 128)], WSTGB[:, cj, ts(ci, 128)], IDBF[:],
                         rd=("WSTGB", "IDBF"), wr=(rb_,))
                p.do("vector", "tensor_copy", WFT[:, ci, ts(half, 512)], bank[:, 0:512],
                     rd=(rb_,), wr=("WFT",))

        wtrans(WSTGB_A, "WSTGB_A", WQT, "WQT", 0)
        wtrans(WSTGB_XB, "WSTGB_XB", WQT, "WQT", 1)
        claim("WSTGB_G1")
        p.do("gpsimd", "dma_start", out=WSTGB_G1[:],
             in_=dw["wv"][512:1024, :].rearrange("(a p) f -> p a f", p=128), wr=("WSTGB_G1",))
        claim("WSTGB_XB")
        p.do("gpsimd", "dma_start", out=WSTGB_XB[:],
             in_=dw["wo"][0:512, :].rearrange("(a p) f -> p a f", p=128), wr=("WSTGB_XB",))
        claim("WSTGB_KD")
        p.do("gpsimd", "dma_start", out=WSTGB_KD[:],
             in_=dw["wo"][512:1024, :].rearrange("(a p) f -> p a f", p=128), wr=("WSTGB_KD",))
        wtrans(WSTGB_G1, "WSTGB_G1", WVT, "WVT", 1)
        claim("WGT")
        wtrans(WSTGB_XB, "WSTGB_XB", WGT, "WGT", 0)
        wtrans(WSTGB_KD, "WSTGB_KD", WGT, "WGT", 1)

        STG = XT[0:1].rearrange("p a f -> p (a f)")
        for si, srcn in enumerate(("bq", "bk", "lng", "fbias")):
            p.do("sync", "dma_start", out=STG[:, si * 1024:(si + 1) * 1024],
                 in_=dw[srcn].rearrange("(a f) -> a f", a=1), wr=("STGROW", "XT"))
        p.do("sync", "dma_start", out=ROWD[0:1, 0:1024], in_=dw["lnb"].rearrange("(a f) -> a f", a=1),
             wr=("m_row", "s_row"))
        for pi in range(3):
            items = [("bq", BQC, "BQC"), ("bk", BKC, "BKC"), ("lng", LNGC, "LNGC"),
                     ("fbias", FBC, "FBC"), ("lnb", None, "LNBCB")][2 * pi: 2 * pi + 2]
            for ci in range(NCH):
                for ii, (srcn, dstc, rdst) in enumerate(items):
                    bank, rb_ = (PC, rPC) if ii == 0 else (PD, rPD)
                    if srcn == "lnb":
                        srcap = ROWD[0:1, ts(ci, 128)]
                        rsrc_ = "m_row"
                    else:
                        si = ("bq", "bk", "lng", "fbias").index(srcn)
                        srcap = STG[:, si * 1024 + ci * 128: si * 1024 + (ci + 1) * 128]
                        rsrc_ = "STGROW"
                    p.do("tensor", "transpose", bank[:, ci:ci + 1], srcap, IDF[0:1, 0:1],
                         rd=(rsrc_, "IDF"), wr=(rb_,))
            for ii, (srcn, dstc, rdst) in enumerate(items):
                bank, rb_ = (PC, rPC) if ii == 0 else (PD, rPD)
                tgt = LNBCB[:] if srcn == "lnb" else dstc[:]
                p.do("vector", "tensor_copy", tgt, bank[:, 0:NCH], rd=(rb_,), wr=(rdst,))

        p.do("sync", "dma_start", out=STG[:, 2048:3072], in_=dw["bo"].rearrange("(a f) -> a f", a=1),
             wr=("STGBO", "STGROW", "XT"))
        for ci in range(NCH):
            for half, (bank, rb_) in enumerate(((PD, rPD), (PC, rPC))):
                p.do("tensor", "matmul", bank[0:1, 0:512], LNBCB[:, ci:ci + 1],
                     WGT[:, ci, ts(half, 512)], start=(ci == 0), stop=(ci == NCH - 1),
                     rd=("LNBCB", "WGT"), wr=(rb_,))
        for half, (bank, rb_) in enumerate(((PD, rPD), (PC, rPC))):
            p.do("vector", "tensor_tensor", STG[:, 1024 + 512 * half: 1024 + 512 * (half + 1)],
                 bank[0:1, 0:512], STG[:, 2048 + 512 * half: 2048 + 512 * (half + 1)], ALU.add,
                 rd=(rb_, "STGBO"), wr=("STGBW", "XT"))
        for ci in range(NCH):
            p.do("tensor", "transpose", PC[:, ci:ci + 1], STG[:, 1024 + 128 * ci: 1152 + 128 * ci], IDF[0:1, 0:1],
                 rd=("STGBW", "IDF"), wr=(rPC,))
        p.do("vector", "tensor_copy", BWOC[:], PC[:, 0:NCH], rd=(rPC,), wr=("BWOC",))
        for ci in range(NCH):
            p.do("vector", "tensor_scalar_mul", WGT[:, ci, :], WGT[:, ci, :], LNGC[:, ci:ci + 1],
                 rd=("WGT", "LNGC"), wr=("WGT",))

        def colsum_rows(wt, rdw, slot, width):
            banks = [(PC, rPC), (PD, rPD)][: width // 512]
            for ci in range(NCH):
                for half, (bank, rb_) in enumerate(banks):
                    p.do("tensor", "matmul", bank[0:1, 0:512], ONESC[:], wt[:, ci, ts(half, 512)],
                         start=(ci == 0), stop=(ci == NCH - 1), rd=(rdw, "ONESC"), wr=(rb_,))
            for half, (bank, rb_) in enumerate(banks):
                p.do("vector", "tensor_copy", rb(slot, half * 512, 512), bank[0:1, 0:512],
                     rd=(rb_,), wr=("ROWB",))
        colsum_rows(WCD, "WCD", cs_wcd, 512)
        colsum_rows(WFT, "WFT", cs_wft, 1024)
        colsum_rows(WQT, "WQT", cs_wq, 1024)
        colsum_rows(WKT, "WKT", cs_wk, 1024)
        colsum_rows(WVT, "WVT", cs_wv, 1024)
        colsum_rows(WGT, "WGT", cs_wg, 1024)

    def load_tile(g, tt):
        buf = tt % 2
        claim(f"XIN{buf}")
        rows = d_x[g * SEG + tt * 128: g * SEG + (tt + 1) * 128, :]
        p.do("gpsimd", "dma_start", out=XIN[:, buf, :], in_=rows, wr=(f"XIN{buf}",))

    def trans_in(g):
        for tt in range(NTT):
            buf = tt % 2
            for c4 in range(2):
                bank, rb_ = (PC, rPC) if (2 * tt + c4) % 2 == 0 else (PD, rPD)
                for ci in range(4 * c4, 4 * c4 + 4):
                    p.do("tensor", "transpose", bank[:, ts(ci % 4, 128)], XIN[:, buf, ts(ci, 128)], IDF[:],
                         rd=(f"XIN{buf}", "IDF"), wr=(rb_,))
                p.do("vector", "tensor_copy",
                     XT[:, 4 * c4:4 * c4 + 4, ts(tt, 128)], bank[:, 0:512].rearrange("p (a f) -> p a f", a=4),
                     rd=(rb_,), wr=("XT",))
            if tt + 2 < NTT:
                load_tile(g, tt + 2)

    def cast_xb():
        claim("XB")
        for ci in range(NCH):
            p.do("vector", "tensor_copy", XB[:, ci, :], XT[:, ci, :], rd=("XT",), wr=("XB",))

    def ln_stats(src, rsrc, extra=False):
        if rsrc == "XB":
            claim("QTB")
        for ci in range(NCH):
            p.do("vector", "tensor_mul", SQ[:, ci, :], src[:, ci, :], src[:, ci, :],
                 rd=(rsrc,), wr=("QTB",))
        srcf = src[:].rearrange("p a f -> p (a f)")
        SQf = SQ[:].rearrange("p a f -> p (a f)")
        for ci in range(NCH):
            p.do("tensor", "matmul", PA[0:1, 0:512], ONESC[:], srcf[:, ci * SEG:(ci + 1) * SEG],
                 start=(ci == 0), stop=(ci == NCH - 1), rd=(rsrc, "ONESC"), wr=(rPA0,))
        for ci in range(NCH):
            p.do("tensor", "matmul", PA[0:1, 512:1024], ONESC[:], SQf[:, ci * SEG:(ci + 1) * SEG],
                 start=(ci == 0), stop=(ci == NCH - 1), rd=("QTB", "ONESC"), wr=(rPA1,))
        p.do("vector", "tensor_scalar_mul", rd_(m_row), PA[0:1, 0:512], 1.0 / C, rd=(rPA0,), wr=("m_row",))
        p.do("vector", "tensor_mul", rd_(s_row), rd_(m_row), rd_(m_row), rd=("m_row",), wr=("s_row",))
        p.do("vector", "scalar_tensor_tensor", rd_(s_row), PA[0:1, 512:1024], 1.0 / C, rd_(s_row),
             ALU.mult, ALU.subtract, rd=(rPA1, "s_row"), wr=("s_row",))
        p.do("scalar", "activation", rd_(s_row), rd_(s_row), AF.Ln, bias=CEPS[0:1, 0:1],
             rd=("s_row", "CEPS"), wr=("s_row",))
        p.do("scalar", "activation", rf(rs_row), rd_(s_row), AF.Exp, scale=CNH[0:1, 0:1],
             rd=("s_row", "CNH"), wr=("rs_row",))
        if extra:
            p.do("vector", "reciprocal", rd_(s_row), rf(rs_row), rd=("rs_row",), wr=("s_row",))
        p.do("vector", "tensor_scalar_mul", rb(negmrs_row), rd_(m_row), -1.0, rd=("m_row",), wr=("negmrs_row",))
        if extra:
            p.do("vector", "tensor_scalar_mul", rf(negmf_row), rd_(m_row), -1.0, rd=("m_row",), wr=("negmf_row",))
        p.do("tensor", "matmul", PB[:, 0:512], rf(onesf_row), rf(rs_row), start=True, stop=True,
             rd=("rs_row", "ROWF"), wr=(rPB0,))
        p.do("vector", "tensor_copy", RSB[:], PB[:, 0:512], rd=(rPB0,), wr=("RSB",))
        if extra:
            p.do("vector", "tensor_copy", rb(invrs_row), rd_(s_row), rd=("s_row",), wr=("invrs_row",))

    def rs_transposed():
        for tt in range(NTT):
            p.do("tensor", "transpose", PC[:, tt:tt + 1], rf(rs_row, tt * 128, 128), IDF[0:1, 0:1],
                 rd=("rs_row", "IDF"), wr=(rPC,))
        p.do("vector", "tensor_copy", RST[:], PC[:, 0:NTT], rd=(rPC,), wr=("RST",))

    def sub1(g):
        cast_xb()
        ln_stats(XB, "XB")
        claim("LATB", "QH0", "QH1")
        for lp in range(LD // 256):
            lm0, lm1 = 2 * lp, 2 * lp + 1
            for ci in range(NCH):
                p.do("tensor", "matmul", PC[:, 0:512], WCDf[:, ci * LD + lm0 * 128: ci * LD + (lm0 + 1) * 128],
                     XBf[:, ci * SEG:(ci + 1) * SEG],
                     start=(ci == 0), stop=False, rd=("WCD", "XB"), wr=(rPC,))
                p.do("tensor", "matmul", PD[:, 0:512], WCDf[:, ci * LD + lm1 * 128: ci * LD + (lm1 + 1) * 128],
                     XBf[:, ci * SEG:(ci + 1) * SEG],
                     start=(ci == 0), stop=False, rd=("WCD", "XB"), wr=(rPD,))
            p.do("tensor", "matmul", PC[:, 0:512], rb(cs_wcd, lm0 * 128, 128), rb(negmrs_row),
                 start=False, stop=True, rd=("ROWB", "negmrs_row"), wr=(rPC,))
            p.do("tensor", "matmul", PD[:, 0:512], rb(cs_wcd, lm1 * 128, 128), rb(negmrs_row),
                 start=False, stop=True, rd=("ROWB", "negmrs_row"), wr=(rPD,))
            p.do("vector", "tensor_tensor", LATB[:, lm0, :], PC[:, 0:512], RSB[:], ALU.mult,
                 rd=(rPC, "RSB"), wr=("LATB",))
            p.do("vector", "tensor_tensor", LATB[:, lm1, :], PD[:, 0:512], RSB[:], ALU.mult,
                 rd=(rPD, "RSB"), wr=("LATB",))
        for dup in range(2):
            for lc in range(LD // 128):
                p.do("tensor", "matmul", PC[64 * dup:64 * dup + 64, 0:512],
                     WKVf[:, lc * 128: lc * 128 + 64], LATBf[:, lc * SEG:(lc + 1) * SEG],
                     start=(lc == 0), stop=(lc == LD // 128 - 1), rd=("WKV", "LATB"), wr=(rPC,))
        if g == 0:
            claim("KDUP")
        p.do("vector", "tensor_copy", KDUP[:, g * SEG:(g + 1) * SEG], PC[:, 0:512], rd=(rPC,), wr=("KDUP",))
        for tt in range(NTT):
            for lc in range(LD // 128):
                p.do("tensor", "matmul", PD[:, ts(tt, 64)],
                     LATBf[:, lc * SEG + tt * 128: lc * SEG + (tt + 1) * 128],
                     WKVf[:, lc * 128 + 64: lc * 128 + 128],
                     start=(lc == 0), stop=(lc == LD // 128 - 1), rd=("WKV", "LATB"), wr=(rPD,))
        p.do("vector", "tensor_copy", VEXT[:, 4 * g:4 * g + 4, 0:64],
             PD[:, 0:256].rearrange("p (a f) -> p a f", a=4), rd=(rPD,), wr=("VEXT",))
        imax = 4 * g + 3
        PQ = PBF[:].bitcast(F32)

        def qgen(pr2):
            qb2 = pr2 % 2
            for lc in range(LD // 128):
                p.do("tensor", "matmul", PQ[:, 0:512],
                     WQUPf[:, lc * C + pr2 * 128: lc * C + (pr2 + 1) * 128],
                     LATBf[:, lc * SEG:(lc + 1) * SEG],
                     start=(lc == 0), stop=(lc == LD // 128 - 1), rd=("WQUP", "LATB"), wr=(rPBF,))
            p.do("vector", "tensor_copy", QH[:, qb2, :], PQ[:, 0:512], rd=(rPBF,), wr=(f"QH{qb2}",))

        def emit_qk(pr2, i):
            qb2 = pr2 % 2
            qe = 2 * (i % 2)
            qo = qe + 1
            p.do("tensor", "matmul", PTAB[:, qe * 512:(qe + 1) * 512],
                 KDUP[0:64, ts(i, 128)], QH[0:64, qb2, :],
                 start=True, stop=True, tile_position=(0, 0), rd=("KDUP", f"QH{qb2}"), wr=(f"PT{qe}",))
            p.do("tensor", "matmul", PTAB[:, qo * 512:(qo + 1) * 512],
                 KDUP[64:128, ts(i, 128)], QH[64:128, qb2, :],
                 start=True, stop=True, tile_position=(64, 0), rd=("KDUP", f"QH{qb2}"), wr=(f"PT{qo}",))

        qk_pre = [False] * (NH // 2)
        for pr in range(NH // 2):
            if pr == 0:
                claim("EXPB0", "EXPB1", "EXPB2", "EXPB3", "YT")
                for tt in range(NTT):
                    claim(f"YN{tt}", f"DDIA{tt}", f"RRC{tt}")
                qgen(0)
            qbuf = pr % 2
            if not qk_pre[pr]:
                for _pre in range(min(2, imax + 1)):
                    emit_qk(pr, _pre)
            qgen_done = False
            i = 0
            while i <= imax:
                ni = 2 if i + 1 <= imax else 1
                for ii in range(i, i + ni):
                    ebx = ii % 4
                    qlo = 2 * (ii % 2)
                    p.do("scalar", "activation", EXPB[:, ebx, :],
                         PTAB[:, qlo * 512: qlo * 512 + 1024], AF.Exp,
                         bias=CEXP[:, 0:1], scale=CEXP[:, 1:2],
                         rd=(f"PT{qlo}", f"PT{qlo + 1}", "CEXP"), wr=(f"EXPB{ebx}",))
                for ii in range(i, i + ni):
                    ebb = ii % 4
                    r = ii - 4 * g
                    if r >= 0:
                        moff = 384 - 128 * r
                        msk = MBIG[:, moff:moff + 512].rearrange("p (a f) -> p a f", a=1).broadcast_to([128, 2, 512])
                        p.do("vector", "tensor_tensor", EXPB[:, ebb, :].rearrange("p (a f) -> p a f", a=2),
                             EXPB[:, ebb, :].rearrange("p (a f) -> p a f", a=2), msk, ALU.mult,
                             rd=(f"EXPB{ebb}", "MBIG"), wr=(f"EXPB{ebb}",))
                for ii in range(i + 2, min(i + 2 + ni, imax + 1)):
                    emit_qk(pr, ii)
                if not qgen_done and i >= max(0, imax // 2 - 1) and pr + 1 < NH // 2:
                    qgen(pr + 1)
                    qgen_done = True
                EXPF = EXPB[:].rearrange("p a f -> p (a f)")
                VEXF = VEXT[:].rearrange("p a f -> p (a f)")
                for ii in range(i, i + ni):
                    ebb = ii % 4
                    for tt in range(NTT):
                        if ii > 4 * g + tt:
                            continue
                        for hh in range(2):
                            ybank, ry = (PC, rPC) if hh == 0 else (PD, rPD)
                            p.do("tensor", "matmul", ybank[:, tt * 66:(tt + 1) * 66],
                                 EXPF[:, ebb * 1024 + hh * 512 + tt * 128: ebb * 1024 + hh * 512 + (tt + 1) * 128],
                                 VEXF[:, ii * 66:(ii + 1) * 66],
                                 start=(ii == 0 and tt == 0), stop=(ii == imax and tt == NTT - 1),
                                 skip_group_check=True,
                                 rd=(f"EXPB{ebb}", "VEXT", "VEXT_ones"), wr=(ry,))
                i += ni
            if pr + 1 < NH // 2 and not qgen_done:
                qgen(pr + 1)
            if pr + 1 < NH // 2:
                for _pre in range(min(2, imax + 1)):
                    emit_qk(pr + 1, _pre)
                qk_pre[pr + 1] = True
            for hh in range(2):
                ybank, ry = (PC, rPC) if hh == 0 else (PD, rPD)
                for tt in range(NTT):
                    p.do("vector", "reciprocal", RRC[:, tt:tt + 1], ybank[:, tt * 66 + 64: tt * 66 + 65],
                         rd=(ry,), wr=(f"RRC{tt}",))
                for tt in range(NTT):
                    p.do("vector", "tensor_copy", YN[:, tt, :], ybank[:, tt * 66:(tt + 1) * 66],
                         rd=(ry,), wr=(f"YN{tt}",))
                    p.do("vector", "tensor_scalar_mul", DDIA[:, tt, :], IDBF[:], RRC[:, tt:tt + 1],
                         rd=("IDBF", f"RRC{tt}"), wr=(f"DDIA{tt}",))
                    p.do("tensor", "matmul", PQ[64 * hh:64 * hh + 64, ts(tt, 128)],
                         YN[:, tt, 0:64], DDIA[:, tt, :], start=True, stop=True,
                         rd=(f"YN{tt}", f"DDIA{tt}"), wr=(rPBF,))
            p.do("vector", "tensor_copy", YT[:, pr, :], PQ[:, 0:512], rd=(rPBF,), wr=("YT",))
        for cp in range(NCH // 2):
            co0, co1 = 2 * cp, 2 * cp + 1
            for ci in range(NCH):
                p.do("tensor", "matmul", PC[:, 0:512],
                     WCPf[:, ci * C + co0 * 128: ci * C + (co0 + 1) * 128],
                     YTf[:, ci * SEG:(ci + 1) * SEG],
                     start=(ci == 0), stop=(ci == NCH - 1), rd=("WCP", "YT"), wr=(rPC,))
                p.do("tensor", "matmul", PD[:, 0:512],
                     WCPf[:, ci * C + co1 * 128: ci * C + (co1 + 1) * 128],
                     YTf[:, ci * SEG:(ci + 1) * SEG],
                     start=(ci == 0), stop=(ci == NCH - 1), rd=("WCP", "YT"), wr=(rPD,))
            p.do("vector", "tensor_add", XT[:, co0, :], XT[:, co0, :], PC[:, 0:512],
                 rd=(rPC, "XT"), wr=("XT",))
            p.do("vector", "tensor_add", XT[:, co1, :], XT[:, co1, :], PD[:, 0:512],
                 rd=(rPD, "XT"), wr=("XT",))

    def sub_ffn(g):
        cast_xb()
        ln_stats(XB, "XB")
        claim("T1F0", "T1F1")
        for cp in range(NCH // 2):
            co0, co1 = 2 * cp, 2 * cp + 1
            for ci in range(NCH):
                p.do("tensor", "matmul", PC[:, 0:512],
                     WFTf[:, ci * C + co0 * 128: ci * C + (co0 + 1) * 128],
                     XBf[:, ci * SEG:(ci + 1) * SEG],
                     start=(ci == 0), stop=False, rd=("WFT", "XB"), wr=(rPC,))
                p.do("tensor", "matmul", PD[:, 0:512],
                     WFTf[:, ci * C + co1 * 128: ci * C + (co1 + 1) * 128],
                     XBf[:, ci * SEG:(ci + 1) * SEG],
                     start=(ci == 0), stop=False, rd=("WFT", "XB"), wr=(rPD,))
            p.do("tensor", "matmul", PC[:, 0:512], rb(cs_wft, co0 * 128, 128), rb(negmrs_row),
                 start=False, stop=True, rd=("ROWB", "negmrs_row"), wr=(rPC,))
            p.do("tensor", "matmul", PD[:, 0:512], rb(cs_wft, co1 * 128, 128), rb(negmrs_row),
                 start=False, stop=True, rd=("ROWB", "negmrs_row"), wr=(rPD,))
            for co, bank, rb_ in ((co0, PC, rPC), (co1, PD, rPD)):
                tb = co % 2
                p.do("vector", "tensor_tensor", T1F[:, tb, :], bank[:, 0:512], RSB[:], ALU.mult,
                     rd=(rb_, "RSB"), wr=(f"T1F{tb}",))
                p.do("vector", "scalar_tensor_tensor", XT[:, co, :], T1F[:, tb, :], FBC[:, co:co + 1],
                     XT[:, co, :], ALU.add, ALU.add,
                     rd=(f"T1F{tb}", "FBC", "XT"), wr=("XT",))

    def sub3(g):
        if not CFG.get("s3proj", True):
            return
        cast_xb()
        ln_stats(XB, "XB", extra=True)
        rs_transposed()
        claim("QTB", "KTB", "ZT0", "ZT1", "ET", "RT")
        for wt, rw, cs, bcol, rbc, dst, rdst in [
            (WQT, "WQT", cs_wq, BQC, "BQC", QTB, "QTB"),
            (WKT, "WKT", cs_wk, BKC, "BKC", KTB, "KTB"),
        ]:
            wtf = wt[:].rearrange("p a f -> p (a f)")
            for cp in range(NCH // 2):
                oc0, oc1 = 2 * cp, 2 * cp + 1
                for ci in range(NCH):
                    p.do("tensor", "matmul", PC[:, 0:512],
                         wtf[:, ci * C + oc0 * 128: ci * C + (oc0 + 1) * 128],
                         XBf[:, ci * SEG:(ci + 1) * SEG],
                         start=(ci == 0), stop=False, rd=(rw, "XB"), wr=(rPC,))
                    p.do("tensor", "matmul", PD[:, 0:512],
                         wtf[:, ci * C + oc1 * 128: ci * C + (oc1 + 1) * 128],
                         XBf[:, ci * SEG:(ci + 1) * SEG],
                         start=(ci == 0), stop=False, rd=(rw, "XB"), wr=(rPD,))
                p.do("tensor", "matmul", PC[:, 0:512], rb(cs, oc0 * 128, 128), rb(negmrs_row),
                     start=False, stop=True, rd=("ROWB", "negmrs_row"), wr=(rPC,))
                p.do("tensor", "matmul", PD[:, 0:512], rb(cs, oc1 * 128, 128), rb(negmrs_row),
                     start=False, stop=True, rd=("ROWB", "negmrs_row"), wr=(rPD,))
                for oc, bank, rb_ in ((oc0, PC, rPC), (oc1, PD, rPD)):
                    zb = oc % 2
                    p.do("vector", "tensor_tensor", ZT[:, zb, :], bank[:, 0:512], RSB[:], ALU.mult,
                         rd=(rb_, "RSB"), wr=(f"ZT{zb}",))
                    p.do("vector", "tensor_scalar", ET[:, 0, :], ZT[:, zb, :], bcol[:, oc:oc + 1], 0.0,
                         ALU.add, ALU.min, rd=(f"ZT{zb}", rbc), wr=("ET",))
                    p.do("scalar", "activation", ET[:, 0, :], ET[:, 0, :], AF.Exp, rd=("ET",), wr=("ET",))
                    p.do("vector", "tensor_scalar", RT[:, 0, :], ZT[:, zb, :], bcol[:, oc:oc + 1], 0.0,
                         ALU.add, ALU.max, rd=(f"ZT{zb}", rbc), wr=("RT",))
                    p.do("vector", "tensor_add", dst[:, oc, :], ET[:, 0, :], RT[:, 0, :],
                         rd=("ET", "RT"), wr=(rdst,))
        if not CFG.get("s3scan", True):
            return
        claim("BM0", "BM1", "SSB", "KNB0", "KNB1", "VNB0", "VNB1")
        for c in range(NTT):
            Cg = 4 * g + c
            kb = c % 2
            for ci in range(NCH):
                p.do("tensor", "matmul", PC[:, 0:512],
                     XBf[:, ci * SEG + c * 128: ci * SEG + (c + 1) * 128],
                     WVTf[:, ci * C: ci * C + 512],
                     start=(ci == 0), stop=False, rd=("WVT", "XB"), wr=(rPC,))
                p.do("tensor", "matmul", PD[:, 0:512],
                     XBf[:, ci * SEG + c * 128: ci * SEG + (c + 1) * 128],
                     WVTf[:, ci * C + 512: ci * C + 1024],
                     start=(ci == 0), stop=False, rd=("WVT", "XB"), wr=(rPD,))
            for half, bank, rb_ in ((0, PC, rPC), (1, PD, rPD)):
                p.do("tensor", "matmul", bank[:, 0:512], rb(negmrs_row, c * 128, 128), rb(cs_wv, half * 512, 512),
                     start=False, stop=False, rd=("ROWB", "negmrs_row"), wr=(rb_,))
                p.do("tensor", "matmul", bank[:, 0:512], rb(invrs_row, c * 128, 128), rb(bv_row, half * 512, 512),
                     start=False, stop=True, rd=("ROWB", "invrs_row"), wr=(rb_,))
                p.do("vector", "tensor_scalar_mul", VNB[:, kb, ts(half, 512)], bank[:, 0:512], RST[:, c:c + 1],
                     rd=(rb_, "RST"), wr=(f"VNB{kb}",))
            for h2 in range(NCH if CFG.get("s3knat", True) else 0):
                p.do("tensor", "transpose", PBF[:, ts(h2, 128)],
                     KTBf[:, h2 * SEG + c * 128: h2 * SEG + (c + 1) * 128], IDBF[:],
                     rd=("KTB", "IDBF"), wr=(rPBF,))
            if CFG.get("s3knat", True):
                p.do("vector", "tensor_copy", KNB[:, kb, :], PBF[:], rd=(rPBF,), wr=(f"KNB{kb}",))
            else:
                p.do("vector", "memset", KNB[:, kb, :], 0.0, wr=(f"KNB{kb}",))
            if Cg > 0:
                p.do("vector", "tensor_copy", SSB[:], SPS[:], rd=(rSPS,), wr=("SSB",))
            else:
                p.do("vector", "memset", SSB[:], 0.0, wr=("SSB",))
            def emit_b(bq4):
                for hh in range(4):
                    h = bq4 * 4 + hh
                    pbase = 64 * (h % 2)
                    bbank, rbb = (PC, rPC) if h % 2 == 0 else (PD, rPD)
                    p.do("tensor", "matmul", bbank[:, (hh // 2) * 128:(hh // 2) * 128 + 128],
                         KTBf[pbase:pbase + 64, (h // 2) * SEG + c * 128: (h // 2) * SEG + (c + 1) * 128],
                         QTBf[pbase:pbase + 64, (h // 2) * SEG + c * 128: (h // 2) * SEG + (c + 1) * 128],
                         start=True, stop=True, tile_position=(pbase, 0), rd=("KTB", "QTB"), wr=(rbb,))
            if CFG.get("s3b", True):
                emit_b(0)
            for bq4 in range(4 if CFG.get("s3b", True) else 0):
                bmb = bq4 % 2
                msk2 = TRIU[:].rearrange("p (a f) -> p a f", a=1).broadcast_to([128, 2, 128])
                BMv = BM[:, bmb].rearrange("p h f -> p (h f)").rearrange("p (a b f) -> p a b f", a=2, b=2)
                p.do("vector", "tensor_tensor", BMv[:, :, 0, :],
                     PC[:, 0:256].rearrange("p (a f) -> p a f", a=2), msk2, ALU.mult,
                     rd=(rPC, "TRIU"), wr=(f"BM{bmb}",))
                p.do("vector", "tensor_tensor", BMv[:, :, 1, :],
                     PD[:, 0:256].rearrange("p (a f) -> p a f", a=2), msk2, ALU.mult,
                     rd=(rPD, "TRIU"), wr=(f"BM{bmb}",))
                if bq4 + 1 < 4:
                    emit_b(bq4 + 1)
                for hh in range(4 if CFG.get("s3intra", True) else 0):
                    h = bq4 * 4 + hh
                    pbase = 64 * (h % 2)
                    oslot = (h // 2) * 128
                    rpb = (rPB0,) if oslot < 512 else (rPB1,)
                    p.do("tensor", "matmul", PB[pbase:pbase + 64, oslot:oslot + 128],
                         VNBf[:, kb * C + h * 64: kb * C + (h + 1) * 64],
                         BM[:].rearrange("p a b f -> p (a b f)")[:, (bq4 % 2) * 512 + hh * 128: (bq4 % 2) * 512 + (hh + 1) * 128],
                         start=True, stop=False, rd=(f"BM{bq4 % 2}", f"VNB{kb}"), wr=rpb)
                    if CFG.get("s3inter", True):
                        p.do("tensor", "matmul", PB[pbase:pbase + 64, oslot:oslot + 128],
                             SSB[pbase:pbase + 64, (h // 2) * 64:(h // 2) * 64 + 64],
                             QTBf[pbase:pbase + 64, (h // 2) * SEG + c * 128: (h // 2) * SEG + (c + 1) * 128],
                             start=False, stop=True, tile_position=(pbase, pbase), rd=("SSB", "QTB"), wr=rpb)
                    else:
                        p.do("tensor", "matmul", PB[pbase:pbase + 64, oslot:oslot + 128],
                             VNB[:, kb, h * 64:(h + 1) * 64], BM[:, bq4 % 2, hh, :],
                             start=False, stop=True, rd=(f"BM{bq4 % 2}", f"VNB{kb}"), wr=rpb)
            for h in range(NH if CFG.get("s3supd", True) else 0):
                pbase = 64 * (h % 2)
                p.do("tensor", "matmul", SPS[pbase:pbase + 64, (h // 2) * 64:(h // 2) * 64 + 64],
                     KNBf[:, kb * C + h * 64: kb * C + (h + 1) * 64],
                     VNBf[:, kb * C + h * 64: kb * C + (h + 1) * 64],
                     start=(Cg == 0 and h == 0), stop=(Cg == T // 128 - 1 and h == NH - 1),
                     skip_group_check=True,
                     rd=(f"KNB{kb}", f"VNB{kb}"), wr=(rSPS,))
            if CFG.get("s3oevac", True):
                p.do("vector", "tensor_copy", XB[:, :, ts(c, 128)],
                     PB[:].rearrange("p (a f) -> p a f", a=8), rd=(rPB0, rPB1), wr=("OB",))
        if not CFG.get("s3out", True):
            return
        ln_stats(XB, "OB", extra=True)
        claim("T1F0", "T1F1")
        p.do("tensor", "matmul", PB[:, 0:512], rf(onesf_row), rf(negmf_row), start=True, stop=True,
             rd=("negmf_row", "ROWF"), wr=(rPB0,))
        p.do("vector", "tensor_copy", T1F[:, 0, :], PB[:, 0:512], rd=(rPB0,), wr=("T1F0",))
        for ci in range(NCH):
            p.do("vector", "tensor_add", XB[:, ci, :], XB[:, ci, :], T1F[:, 0, :],
                 rd=("OB", "T1F0"), wr=("OB",))
        for cp in range(NCH // 2):
            co0, co1 = 2 * cp, 2 * cp + 1
            for ci in range(NCH):
                p.do("tensor", "matmul", PC[:, 0:512],
                     WGTf[:, ci * C + co0 * 128: ci * C + (co0 + 1) * 128],
                     XBf[:, ci * SEG:(ci + 1) * SEG],
                     start=(ci == 0), stop=(ci == NCH - 1), rd=("WGT", "OB"), wr=(rPC,))
                p.do("tensor", "matmul", PD[:, 0:512],
                     WGTf[:, ci * C + co1 * 128: ci * C + (co1 + 1) * 128],
                     XBf[:, ci * SEG:(ci + 1) * SEG],
                     start=(ci == 0), stop=(ci == NCH - 1), rd=("WGT", "OB"), wr=(rPD,))
            for co, bank, rb_ in ((co0, PC, rPC), (co1, PD, rPD)):
                tb = co % 2
                p.do("vector", "tensor_tensor", T1F[:, tb, :], bank[:, 0:512], RSB[:], ALU.mult,
                     rd=(rb_, "RSB"), wr=(f"T1F{tb}",))
                p.do("vector", "scalar_tensor_tensor", XT[:, co, :], T1F[:, tb, :], BWOC[:, co:co + 1],
                     XT[:, co, :], ALU.add, ALU.add,
                     rd=(f"T1F{tb}", "BWOC", "XT"), wr=("XT",))

    def trans_out(g):
        claim("XONT0", "XONT1")
        for tt in range(NTT):
            xb_ = tt % 2
            for half in range(2):
                bank, rb_ = (PC, rPC) if half == 0 else (PD, rPD)
                for ci in range(4 * half, 4 * half + 4):
                    p.do("tensor", "transpose", bank[:, ts(ci % 4, 128)], XT[:, ci, ts(tt, 128)], IDF[:],
                         rd=("XT", "IDF"), wr=(rb_,))
                p.do("vector", "tensor_copy", XON[:, xb_, ts(half, 512)], bank[:, 0:512],
                     rd=(rb_,), wr=(f"XONT{xb_}",))
            rows = d_out[g * SEG + tt * 128: g * SEG + (tt + 1) * 128, :]
            p.do("sync", "dma_start", out=rows, in_=XON[:, xb_, :], rd=(f"XONT{xb_}",), wr=("outdma",))

    startup()
    for g in range(NSEG):
        trans_in(g)
        if CFG["sub1"]:
            sub1(g)
        if CFG["sub2"]:
            sub_ffn(g)
        if CFG["sub3"]:
            sub3(g)
        if g + 1 < NSEG:
            load_tile(g + 1, 0)
            load_tile(g + 1, 1)
        if CFG["sub4"]:
            sub_ffn(g)
        trans_out(g)
    p.do("sync", "dma_start", out=d_dbgb, in_=ROWB[:], rd=("ROWB", "negmrs_row"), wr=("dbg",))
    p.do("sync", "dma_start", out=d_dbgw, in_=WFT[:, 0, :], rd=("WFT",), wr=("dbg",))
    p.do("sync", "dma_start", out=d_dbgo, in_=XB[:], rd=("OB", "XB"), wr=("dbg",))
    p.do("sync", "dma_start", out=d_dbgs, in_=SSB[:], rd=("SSB",), wr=("dbg",))
    p.do("sync", "dma_start", out=d_dbga, in_=DBGS[:, 0:9], rd=("DBGS",), wr=("dbg",))
    for i, (t, r) in enumerate([(DELC[:], "DELC"), (NDELC[:], "NDELC"), (ALFC[:], "ALFC"), (BQC[:, 0:1], "BQC"),
                                (FBC[:, 0:1], "FBC"), (BWOC[:, 0:1], "BWOC"), (RSB[:, 0:1], "RSB"),
                                (RST[:, 0:1], "RST")]):
        p.do("sync", "dma_start", out=d_dbgf[:, i:i + 1], in_=t, rd=(r,), wr=("dbg",),
             allow_slow_non_contiguous=True)
    final = p.ops["sync"][-1]
    assert final.fn == "dma_start"
    p.barrier_wait("sync", final)

    p.finalize_and_emit()
    return nc


_NC_CACHE = None


def make_consts():
    s = np.arange(128)[:, None]
    u = np.arange(896)[None, :]
    mbig = (s <= (u - 384)).astype(ml_dtypes.bfloat16)
    tj = np.arange(128)[:, None]
    ti = np.arange(128)[None, :]
    triu = (tj <= ti).astype(ml_dtypes.bfloat16)
    idm = np.eye(128).astype(ml_dtypes.bfloat16)
    cbf = np.concatenate([mbig, triu, idm], axis=1)
    cf32 = np.eye(128, dtype=np.float32)
    return cbf, cf32


def kernel(L0_attn_a_x, L0_attn_a_Wc_down, L0_attn_a_Wq_up, L0_attn_a_Wkv_up, L0_attn_a_Wc_proj,
           L0_attn_b_Wq, L0_attn_b_bq, L0_attn_b_Wk, L0_attn_b_bk, L0_attn_b_Wv, L0_attn_b_bv,
           L0_attn_b_Wo, L0_attn_b_bo, L0_attn_b_ln_g, L0_attn_b_ln_b,
           L0_ffn_weight, L0_ffn_alpha, L0_ffn_bias):
    global _NC_CACHE
    x = np.ascontiguousarray(np.asarray(L0_attn_a_x, dtype=np.float32))
    B = x.shape[0]
    cbf, cf32 = make_consts()
    weights = dict(
        wcd=np.ascontiguousarray(L0_attn_a_Wc_down, np.float32),
        wqup=np.ascontiguousarray(L0_attn_a_Wq_up, np.float32),
        wkv=np.ascontiguousarray(L0_attn_a_Wkv_up, np.float32),
        wcp=np.ascontiguousarray(L0_attn_a_Wc_proj, np.float32),
        wq=np.ascontiguousarray(L0_attn_b_Wq, np.float32),
        wk=np.ascontiguousarray(L0_attn_b_Wk, np.float32),
        wv=np.ascontiguousarray(L0_attn_b_Wv, np.float32),
        wo=np.ascontiguousarray(L0_attn_b_Wo, np.float32),
        bq=np.ascontiguousarray(L0_attn_b_bq, np.float32),
        bk=np.ascontiguousarray(L0_attn_b_bk, np.float32),
        bv=np.ascontiguousarray(L0_attn_b_bv, np.float32),
        bo=np.ascontiguousarray(L0_attn_b_bo, np.float32),
        lng=np.ascontiguousarray(L0_attn_b_ln_g, np.float32),
        lnb=np.ascontiguousarray(L0_attn_b_ln_b, np.float32),
        fw=np.ascontiguousarray(L0_ffn_weight, np.float32),
        falpha=np.ascontiguousarray(L0_ffn_alpha, np.float32),
        fbias=np.ascontiguousarray(L0_ffn_bias, np.float32),
        cbf=cbf, cf32=cf32,
    )
    if _NC_CACHE is None:
        _NC_CACHE = build_program()
    nc = _NC_CACHE
    in_maps = []
    for b in range(B):
        m = dict(weights)
        m["x"] = np.ascontiguousarray(x[b])
        in_maps.append(m)
    res = run_bass_kernel_spmd(nc, in_maps, core_ids=list(range(B)))
    out = np.stack([r["out"] for r in res.results], axis=0)
    return out
